# version 14
# baseline (speedup 1.0000x reference)
import numpy as np
import ml_dtypes
from contextlib import ExitStack

import concourse.bass as bass
import concourse.mybir as mybir
from concourse.bass_utils import run_bass_kernel_spmd

F32 = mybir.dt.float32
BF16 = mybir.dt.bfloat16
AF = mybir.ActivationFunctionType
ALU = mybir.AluOpType
AXX = mybir.AxisListType.X

PHASES = 3
DEBUG_DUMP = False
NCORES = 8
T = 4096
D = 1024
NCH = 8
NST = 8
SW = 512
NPAIR = 8
MK = 32
EPS = 1e-5
M32 = float(3 * 2**22)
M64 = float(3 * 2**51)
QMAX = 127.0
KB = 1024


A0 = 20480


def off_x(s, c):
    return A0 + (112 - 16 * s) * KB + c * 2048


def off_O(p, s):
    return A0 + (7 - s) * 8192 + p * 1024


PA = A0 + 65536
OFF_K, OFF_V, OFF_Q = PA, PA + 8192, PA + 16384
OFF_P0, OFF_P1 = PA + 24576, PA + 26624
OFF_RB = PA + 28672
OFF_WQ, OFF_WK, OFF_WV = PA + 30720, PA + 34816, PA + 38912
OFF_VF = PA + 43008
OFF_A2B2 = PA

B0 = A0 + 131072
OFF_XN = B0
OFF_T = B0
OFF_WF = B0 + 16384
OFF_X2BF = B0 + 8192
OFF_X2BF2 = B0 + 16384
OFF_XBF2 = B0 + 24576
OFF_SUMS = B0 + 49152
OFF_QS0 = B0 + 49152
OFF_XQ = B0 + 55296
OFF_XSTG = B0 + 0
OFF_PSTG = B0
OFF_WP = B0 + 32768
OFF_OSTG = B0 + 57344

PR = A0 + 196608
OFF_IDF = PR
OFF_IDB = PR + 512
OFF_ONESC = PR + 768
OFF_ON128 = PR + 1280
OFF_ZO = PR + 1536
OFF_GT = PR + 1568
OFF_BT = PR + 1600
OFF_GS = PR + 1632
OFF_BS = PR + 1664
OFF_BP = PR + 1696
OFF_SCAL = PR + 1728
OFF_WONE = PR + 1760
OFF_MX = PR + 1792
OFF_PAY = PR + 1824
OFF_TRA = PR + 3872
OFF_TRB = PR + 4896
OFF_SROW = PR + 5920
OFF_AWS = PR + 6048
OFF_MROW = PR + 6048
OFF_ROWA = PR + 8096
OFF_RES = PR + 8096
OFF_ROWB = PR + 10144


class Prog:
    ENGINES = ("sync", "gpsimd", "vector", "scalar", "tensor")

    def __init__(self, nc):
        self.nc = nc
        self.streams = {e: [] for e in self.ENGINES}
        self.sems = {}
        self.val = {}
        self.wm = {e: {} for e in self.ENGINES}

    def mksem(self, stack, names):
        for n in names:
            self.sems[n] = stack.enter_context(self.nc.semaphore(n))
            self.val[n] = 0

    def op(self, eng, fn, waits=(), inc=None, inc_n=1, nodrain=False):
        wm = self.wm[eng]
        eff = []
        for (s, v) in waits:
            if v > 0 and wm.get(s, 0) < v:
                eff.append((s, v))
                wm[s] = v
        if inc is not None:
            self.val[inc] += inc_n
        eff = tuple(eff)
        sems = self.sems

        drain = eng in ("vector", "scalar") and not nodrain

        def run(eobj, eff=eff, fn=fn, inc=inc, inc_n=inc_n, drain=drain):
            if drain:
                eobj.drain()
            for (s, v) in eff:
                eobj.wait_ge(sems[s], v)
            last = fn(eobj)
            if inc is not None:
                last.then_inc(sems[inc], inc_n)
        self.streams[eng].append(run)
        return self.val[inc] if inc is not None else None


def build_nc():
    nc = bass.Bass("TRN2", target_bir_lowering=False,
                   detect_race_conditions=False)

    def sbt(name, off, shape, dtype):
        return nc.alloc_sbuf_tensor_at(name, shape, dtype, offset=off)

    x_h = nc.dram_tensor("x", [T, D], F32, kind="ExternalInput").ap()
    wq_h = nc.dram_tensor("wq", [128, 8192], F32, kind="ExternalInput").ap()
    wk_h = nc.dram_tensor("wk", [128, 8192], F32, kind="ExternalInput").ap()
    wv_h = nc.dram_tensor("wv", [128, 8192], F32, kind="ExternalInput").ap()
    wp_h = nc.dram_tensor("wp", [128, 8192], F32, kind="ExternalInput").ap()
    wf_h = nc.dram_tensor("wf", [128, 8192], F32, kind="ExternalInput").ap()
    gt_h = nc.dram_tensor("gt", [128, 8], F32, kind="ExternalInput").ap()
    bt_h = nc.dram_tensor("bt", [128, 8], F32, kind="ExternalInput").ap()
    bp_h = nc.dram_tensor("bpv", [128, 8], F32, kind="ExternalInput").ap()
    idf_h = nc.dram_tensor("idf", [128, 128], F32, kind="ExternalInput").ap()
    idb_h = nc.dram_tensor("idb", [128, 128], BF16, kind="ExternalInput").ap()
    onc_h = nc.dram_tensor("onc", [1, 128], F32, kind="ExternalInput").ap()
    on128_h = nc.dram_tensor("on128", [128, 128], BF16, kind="ExternalInput").ap()
    zo_h = nc.dram_tensor("zo", [128, 2], BF16, kind="ExternalInput").ap()
    out_h = nc.dram_tensor("out", [T, D], F32, kind="ExternalOutput").ap()
    xsp_h = nc.dram_tensor("xspill", [64, 65536], F32).ap()
    tsp_h = nc.dram_tensor("tspill", [128, 8192], BF16).ap()
    cc_in = [nc.dram_tensor(f"cc_in{i}", [1, 512], F32) for i in range(2)]
    cc_out = [nc.dram_tensor(f"cc_out{i}", [1, 512], F32) for i in range(2)]

    xt = {(s, c): sbt(f"x_{s}_{c}", off_x(s, c), [128, SW], F32)
          for s in range(NST) for c in range(NCH)}
    Ot = {(p, s): sbt(f"O_{p}_{s}", off_O(p, s), [128, SW], BF16)
          for p in range(NPAIR) for s in range(NST)}
    Kp = sbt("Kp", OFF_K, [128, 4096], BF16)
    Vp = sbt("Vp", OFF_V, [128, 4096], BF16)
    Qp = sbt("Qp", OFF_Q, [128, 4096], BF16)
    Pb = [sbt(f"Pb{i}", OFF_P0 + i * 2048, [128, 1024], BF16) for i in range(2)]
    Pb2 = [sbt(f"Pc{i}", PA + 51200 + i * 2048, [128, 1024], BF16)
           for i in range(2)]
    Pbb = [[Pb[0], Pb2[0]], [Pb[1], Pb2[1]]]
    RB = sbt("RB", OFF_RB, [128, SW], F32)
    WQs = [sbt(f"WQs{i}", OFF_WQ + i * 2048, [128, 1024], BF16) for i in range(2)]
    WKs = [sbt(f"WKs{i}", OFF_WK + i * 2048, [128, 1024], BF16) for i in range(2)]
    WVs = [sbt(f"WVs{i}", OFF_WV + i * 2048, [128, 1024], BF16) for i in range(2)]
    VF = sbt("VF", OFF_VF, [128, 4096], BF16)
    A2B2 = sbt("A2B2", OFF_A2B2, [2, 4096], F32)

    XN = [sbt(f"XN{c}", OFF_XN + c * 8192, [128, 4096], BF16) for c in range(NCH)]
    TF = sbt("TF", OFF_T, [128, 8192], BF16)
    WFS = sbt("WFS", OFF_WF, [128, 8192], F32)
    X2BF = sbt("X2BF", OFF_X2BF, [128, 4096], BF16)
    X2BF2 = sbt("X2BF2", OFF_X2BF2, [128, 4096], BF16)
    XBF2 = sbt("XBF2", OFF_XBF2, [128, 4096], BF16)
    SUMS = sbt("SUMS", OFF_SUMS, [2, 4096], F32)
    XQ = sbt("XQ", OFF_XQ, [128, 4096], BF16)
    QS = [sbt(f"QS{i}", OFF_QS0 + i * 2048, [128, SW], F32) for i in range(2)]
    XSTG = [sbt(f"XSTG{i}", OFF_XSTG + i * 4096, [128, 1024], F32) for i in range(2)]
    PSTG = [sbt(f"PSTG{i}", OFF_PSTG + i * 16384, [128, 4096], F32) for i in range(2)]
    WP = sbt("WP", OFF_WP, [128, 8192], BF16)
    OSTG = [sbt(f"OSTG{i}", OFF_OSTG + i * 4096, [128, 1024], F32) for i in range(2)]

    IDF = sbt("IDF", OFF_IDF, [128, 128], F32)
    IDB = sbt("IDB", OFF_IDB, [128, 128], BF16)
    ONC = sbt("ONC", OFF_ONESC, [1, 128], F32)
    ON128 = sbt("ON128", OFF_ON128, [128, 128], BF16)
    ZO = sbt("ZO", OFF_ZO, [128, 2], BF16)
    GT = sbt("GT", OFF_GT, [128, 8], F32)
    BT = sbt("BT", OFF_BT, [128, 8], F32)
    GS = sbt("GS", OFF_GS, [128, 8], F32)
    BS = sbt("BS", OFF_BS, [128, 8], F32)
    BP = sbt("BP", OFF_BP, [128, 8], F32)
    SCAL = sbt("SCAL", OFF_SCAL, [128, 8], F32)
    WONE = sbt("WONE", OFF_WONE, [128, 1], F32)
    MX = sbt("MX", OFF_MX, [128, 1], F32)
    PAY = sbt("PAY", OFF_PAY, [1, 512], F32)
    RES = sbt("RES", OFF_RES, [1, 512], F32)
    MROW = sbt("MROW", OFF_MROW, [1, 128], F32)
    TRA = sbt("TRA", OFF_TRA, [128, 256], F32)
    TRB = sbt("TRB", OFF_TRB, [128, 256], F32)
    SROW = sbt("SROW", OFF_SROW, [1, 32], F32)
    ROWA = sbt("ROWA", OFF_ROWA, [1, 512], F32)
    ROWB = sbt("ROWB", OFF_ROWB, [1, 512], F32)

    sAB = [nc.alloc_psum_tensor(f"sAB{i}", [128, 1024], F32) for i in range(2)]
    aA = nc.alloc_psum_tensor("aA", [128, 512], F32)
    aB = nc.alloc_psum_tensor("aB", [128, 512], F32)
    bA = nc.alloc_psum_tensor("bA", [128, 512], F32)
    bB = nc.alloc_psum_tensor("bB", [128, 512], F32)
    pb = [sAB[0][:, 0:512], sAB[0][:, 512:1024],
          sAB[1][:, 0:512], sAB[1][:, 512:1024],
          aA[:, :], aB[:, :], bA[:, :], bB[:, :]]
    accO = [aA, bA]
    accR = [aB, bB]

    with ExitStack() as stack:
        P = Prog(nc)
        P.mksem(stack, ["CC", "Tc", "Vc", "Sc", "DO",
                        "tsc", "tav", "sxp", "vev", "twin", "vwin"]
                + [f"DS{i}" for i in range(8)] + [f"DG{i}" for i in range(4)])
        dma_rr = {"sync": 0, "gpsimd": 0}
        dma_nch = {"sync": 8, "gpsimd": 4}
        dma_pref = {"sync": "DS", "gpsimd": "DG"}
        block = stack.enter_context(nc.Block())

        def dma(queue, out_ap, in_ap, waits=()):
            k = dma_rr[queue]
            dma_rr[queue] = (k + 1) % dma_nch[queue]
            ch = f"{dma_pref[queue]}{k}"
            gate = P.val[ch]
            w = list(waits) + ([(ch, gate)] if gate else [])
            P.op(queue, lambda e, o=out_ap, i=in_ap: e.dma_start(out=o, in_=i),
                 waits=w, inc=ch, inc_n=16)
            return (ch, P.val[ch])

        def vop(fn, waits=()):
            return P.op("vector", fn, waits=waits, inc="Vc")

        def sop(fn, waits=()):
            return P.op("scalar", fn, waits=waits, inc="Sc")

        def top(fn, waits=()):
            return P.op("tensor", fn, waits=waits, inc="Tc")

        tk_idf = dma("gpsimd", IDF[:, :], idf_h)
        consts_tokens = [tk_idf,
                         dma("gpsimd", IDB[:, :], idb_h),
                         dma("gpsimd", ONC[0:1, :], onc_h),
                         dma("gpsimd", ON128[:, :], on128_h),
                         dma("gpsimd", ZO[:, :], zo_h),
                         dma("gpsimd", GT[:, :], gt_h),
                         dma("gpsimd", BT[:, :], bt_h),
                         dma("gpsimd", BP[:, :], bp_h),
                         dma("gpsimd", WFS[:, :], wf_h)]

        vop(lambda e: e.memset(WONE[:, 0:1], 1.0))
        vop(lambda e: e.tensor_scalar(out=SCAL[:, 0:1], in0=WONE[:, 0:1], scalar1=M64,
                                      scalar2=M64, op0=ALU.add, op1=ALU.subtract))
        v_chi = P.val["Vc"]

        st_mm = {}
        sums_ev = {}

        def stats_reset():
            st_mm.clear()
            sums_ev.clear()

        def stats_strip(s, gates, x2b, xb=None):
            bk = pb[4 + (s % 2)]
            for c in range(NCH):
                tw = list(gates) if c == 0 else []
                if (s - 1, c) in st_mm:
                    tw.append(("Tc", st_mm[(s - 1, c)]))
                s1 = sop(lambda e, s=s, c=c, x2b=x2b: e.square(
                    x2b[:, c * SW:(c + 1) * SW], xt[(s, c)][:, :]), waits=tw)
                mw = [("Sc", s1)]
                if xb is None:
                    ones_mm = (lambda e, s=s, c=c, bk=bk: e.matmul(
                        bk[0:1, :], WONE[:, 0:1], xt[(s, c)][:, :],
                        start=False, stop=False))
                else:
                    v1 = vop(lambda e, s=s, c=c, xb=xb: e.tensor_copy(
                        xb[:, c * SW:(c + 1) * SW], xt[(s, c)][:, :]), waits=tw)
                    mw.append(("Vc", v1))
                    ones_mm = (lambda e, c=c, bk=bk, xb=xb: e.matmul(
                        bk[0:1, :], ON128[:, 0:1], xb[:, c * SW:(c + 1) * SW],
                        start=False, stop=False))
                if c == 0:
                    mw.append(("Vc", sums_ev.get(s - 2, 0)))
                    top(lambda e, c=c, bk=bk, x2b=x2b: e.matmul(
                        bk[0:2, :], ZO[:, :], x2b[:, c * SW:(c + 1) * SW],
                        start=True, stop=False), waits=mw)
                    st_mm[(s, c)] = top(ones_mm)
                elif c < NCH - 1:
                    top(lambda e, c=c, bk=bk, x2b=x2b: e.matmul(
                        bk[0:2, :], ZO[:, :], x2b[:, c * SW:(c + 1) * SW],
                        start=False, stop=False), waits=mw)
                    st_mm[(s, c)] = top(ones_mm)
                else:
                    top(ones_mm, waits=mw)
                    st_mm[(s, c)] = top(lambda e, c=c, bk=bk, x2b=x2b: e.matmul(
                        bk[0:2, :], ZO[:, :], x2b[:, c * SW:(c + 1) * SW],
                        start=False, stop=True))
            sums_ev[s] = vop(lambda e, s=s, bk=bk: e.tensor_copy(
                SUMS[0:2, s * SW:(s + 1) * SW], bk[0:2, :]),
                waits=[("Tc", st_mm[(s, NCH - 1)])])

        xr = x_h.rearrange("(n p) d -> n p d", p=128)
        v_round = {}
        t_round = {}
        stats_reset()
        for tc in range(32):
            buf = XSTG[tc % 2]
            qt = [dma("sync", buf[:, q2 * 256:(q2 + 1) * 256],
                      xr[tc][:, q2 * 256:(q2 + 1) * 256],
                      waits=[("Tc", t_round.get(tc - 2, 0))])
                  for q2 in range(4)]
            s, q = tc // 4, tc % 4
            slots = []
            tv = 0
            bw = [tk_idf] if tc < 2 else [tk_idf, ("vwin", v_round[tc - 2])]
            for c in range(NCH):
                bank = pb[2 * (tc % 2) + c // 4]
                slot = bank[:, (c % 4) * 128:(c % 4) * 128 + 128]
                tv = top(lambda e, o=slot, i=buf[:, c * 128:(c + 1) * 128]:
                         e.matmul(o, i, IDF[:, :], is_transpose=True,
                                  start=True, stop=True),
                         waits=[qt[c // 2]] + bw)
                bw = []
                slots.append(slot)
            t_round[tc] = tv
            last_v = None
            for c in range(NCH):
                last_v = P.op("vector",
                              lambda e, o=xt[(s, c)][:, q * 128:(q + 1) * 128],
                              i=slots[c]:
                              e.tensor_copy(o, i), waits=[("Tc", tv)], inc="vwin")
            v_round[tc] = last_v
            if q == 3:
                stats_strip(s, consts_tokens + [("vwin", v_round[tc])], X2BF)
        x_loaded = P.val["vwin"]
        t_xload = P.val["Tc"]

        AWS = sbt("AWS", OFF_AWS, [128, 512], F32)
        AWS2 = sbt("AWS2", OFF_XQ, [128, 512], F32)

        def emit_mask():
            v0 = P.val["Vc"]
            t_saw = 0
            for k in range(16):
                sa = sop(lambda e, k=k: e.activation(
                    AWS[:, :], WFS[:, k * 512:(k + 1) * 512], AF.Abs),
                    waits=consts_tokens + [("Vc", max(v_chi, v0)),
                                           ("Tc", t_saw)])
                t_saw = top(lambda e, k=k: e.matmul(
                    pb[4][0:1, :], WONE[:, 0:1], AWS[:, :],
                    start=(k == 0), stop=(k == 15)), waits=[("Sc", sa)])
            t_sumaw = P.val["Tc"]
            vop(lambda e: e.reduce_sum(SROW[0:1, 0:1], pb[4][0:1, :], AXX),
                waits=[("Tc", t_sumaw)])
            v_d = vop(lambda e: e.tensor_scalar(
                out=SROW[0:1, 1:2], in0=SROW[0:1, 0:1],
                scalar1=float(2.0**-20), scalar2=0.7,
                op0=ALU.mult, op1=ALU.mult))
            tv = top(lambda e: e.matmul(pb[5][0:128, 0:1], ONC[0:1, :],
                                        SROW[0:1, 1:2], start=True, stop=True),
                     waits=[("Vc", v_d)])
            v_db = vop(lambda e: e.tensor_copy(SCAL[:, 3:4], pb[5][0:128, 0:1]),
                       waits=[("Tc", tv)])
            v_tf = v_db
            t_cnt = 0
            for k in range(16):
                sa = sop(lambda e, k=k: e.activation(
                    AWS[:, :], WFS[:, k * 512:(k + 1) * 512], AF.Abs),
                    waits=[("Vc", v_tf)])
                va = vop(lambda e: e.scalar_tensor_tensor(
                    out=AWS2[:, :], in0=AWS[:, :], scalar=SCAL[:, 3:4],
                    in1=AWS[:, :], op0=ALU.is_gt, op1=ALU.mult),
                    waits=[("Sc", sa), ("Tc", t_cnt)])
                top(lambda e, k=k: e.matmul(pb[7][0:1, :], WONE[:, 0:1],
                                            AWS2[:, :],
                                            start=(k == 0), stop=(k == 15)),
                    waits=[("Vc", va)])
                vb = vop(lambda e: e.tensor_scalar(
                    out=AWS2[:, :], in0=AWS[:, :], scalar1=SCAL[:, 3:4],
                    scalar2=None, op0=ALU.is_gt),
                    waits=[("Tc", P.val["Tc"])])
                t_cnt = top(lambda e, k=k: e.matmul(
                    pb[6][0:1, :], WONE[:, 0:1], AWS2[:, :],
                    start=(k == 0), stop=(k == 15)), waits=[("Vc", vb)])
                sg = sop(lambda e, k=k: e.activation(
                    AWS[:, :], WFS[:, k * 512:(k + 1) * 512], AF.Sign),
                    waits=[("Vc", vb)])
                v_tf = vop(lambda e, k=k: e.tensor_tensor(
                    TF[:, k * 512:(k + 1) * 512], AWS[:, :], AWS2[:, :],
                    ALU.mult),
                    waits=[("Sc", sg), ("Tc", max(t_cnt, t_xload))])
            t_maw = P.val["Tc"]
            vop(lambda e: e.reduce_sum(SROW[0:1, 2:3], pb[6][0:1, :], AXX),
                waits=[("Tc", t_maw)])
            vop(lambda e: e.reduce_sum(SROW[0:1, 3:4], pb[7][0:1, :], AXX))
            vop(lambda e: e.tensor_scalar(out=SROW[0:1, 2:3], in0=SROW[0:1, 2:3],
                                          scalar1=1.0, scalar2=None, op0=ALU.max))
            vop(lambda e: e.reciprocal(SROW[0:1, 2:3], SROW[0:1, 2:3]))
            vop(lambda e: e.tensor_tensor(SROW[0:1, 4:5], SROW[0:1, 3:4],
                                          SROW[0:1, 2:3], ALU.mult))

        def ap2(tensor, off_elems, step, count):
            return bass.AP(tensor, off_elems, [[256, 128], [step, count]])

        def emit_stats2(need_r2, first_waits):
            stats_reset()
            fw = list(first_waits)
            for s in range(NST):
                stats_strip(s, fw, X2BF2, xb=XBF2)
                fw = []
            return stats_tail(need_r2)

        def stats_tail(need_r2, extra_tw=()):
            tw = [("Vc", sums_ev[NST - 1])] + list(extra_tw)
            for j in range(32):
                top(lambda e, j=j: e.matmul(pb[0][:, 2 * j:2 * j + 2],
                                            SUMS[0:2, j * 128:(j + 1) * 128],
                                            IDF[0:2, 0:2], is_transpose=True,
                                            start=True, stop=True), waits=tw)
                tw = ()
            t_tr = P.val["Tc"]
            vop(lambda e: e.tensor_copy(TRA[:, 0:64], pb[0][:, 0:64]),
                waits=[("Tc", t_tr)])

            s1 = ap2(TRA, 0, 2, 32)
            s2 = ap2(TRA, 1, 2, 32)
            mu = ap2(TRA, 64, 2, 32)
            var = ap2(TRA, 65, 2, 32)
            r2ap = ap2(TRA, 128, 2, 32)
            av = ap2(TRB, 0, 2, 32)
            bv = ap2(TRB, 1, 2, 32)
            r0 = ap2(TRB, 64, 2, 32)
            tt = ap2(TRB, 65, 2, 32)
            srcbuf = ap2(TRB, 128, 2, 32)

            def rsqrt_into(dst):
                vr = vop(lambda e: e.reciprocal(r0, srcbuf))
                sv = sop(lambda e: e.activation(dst, r0, AF.Sqrt), waits=[("Vc", vr)])
                for it in range(2):
                    vop(lambda e: e.tensor_tensor(tt, dst, dst, ALU.mult),
                        waits=[("Sc", sv)])
                    vop(lambda e: e.tensor_tensor(tt, tt, srcbuf, ALU.mult))
                    vop(lambda e: e.tensor_scalar(out=tt, in0=tt, scalar1=-0.5,
                                                  scalar2=1.5, op0=ALU.mult,
                                                  op1=ALU.add))
                    vop(lambda e: e.tensor_tensor(dst, dst, tt, ALU.mult))

            vop(lambda e: e.tensor_scalar(out=mu, in0=s1, scalar1=float(1.0 / D),
                                          scalar2=None, op0=ALU.mult))
            vop(lambda e: e.tensor_scalar(out=var, in0=s2, scalar1=float(1.0 / D),
                                          scalar2=None, op0=ALU.mult))
            vop(lambda e: e.tensor_tensor(tt, mu, mu, ALU.mult))
            vop(lambda e: e.tensor_tensor(var, var, tt, ALU.subtract))
            vop(lambda e: e.tensor_scalar(out=srcbuf, in0=var, scalar1=EPS,
                                          scalar2=None, op0=ALU.add))
            rsqrt_into(av)
            if need_r2:
                vop(lambda e: e.tensor_tensor(tt, av, av, ALU.mult))
                vop(lambda e: e.tensor_tensor(tt, tt, var, ALU.mult))
                vop(lambda e: e.tensor_scalar(out=srcbuf, in0=tt, scalar1=EPS,
                                              scalar2=None, op0=ALU.add))
                rsqrt_into(r2ap)
                vop(lambda e: e.tensor_tensor(av, av, r2ap, ALU.mult))
            vop(lambda e: e.tensor_tensor(bv, mu, av, ALU.mult))
            v_ab0 = vop(lambda e: e.tensor_scalar(out=bv, in0=bv, scalar1=-1.0,
                                                  scalar2=None, op0=ALU.mult))
            return v_ab0

        def make_rows(srow, v_trb, bank_a, bank_b, extra_waits=(), inv_s=None):
            tw = [("Vc", v_trb)] + list(extra_waits)
            for jj in range(4):
                j = 4 * srow + jj
                top(lambda e, j=j, jj=jj, bank_a=bank_a: e.matmul(
                    bank_a[0:1, jj * 128:(jj + 1) * 128],
                    TRB[:, 2 * j:2 * j + 1], IDF[:, :], is_transpose=True,
                    start=True, stop=True), waits=tw)
                tw = ()
                top(lambda e, j=j, jj=jj, bank_b=bank_b: e.matmul(
                    bank_b[0:1, jj * 128:(jj + 1) * 128],
                    TRB[:, 2 * j + 1:2 * j + 2], IDF[:, :], is_transpose=True,
                    start=True, stop=True))
            t_bk = P.val["Tc"]
            if inv_s is None:
                vop(lambda e, bank_a=bank_a: e.tensor_copy(ROWA[0:1, :], bank_a[0:1, :]),
                    waits=[("Tc", t_bk)])
                vv = vop(lambda e, bank_b=bank_b: e.tensor_copy(ROWB[0:1, :], bank_b[0:1, :]))
            else:
                vop(lambda e, bank_a=bank_a: e.tensor_scalar(
                    out=ROWA[0:1, :], in0=bank_a[0:1, :], scalar1=inv_s,
                    scalar2=None, op0=ALU.mult), waits=[("Tc", t_bk)])
                vv = vop(lambda e, bank_b=bank_b: e.tensor_scalar(
                    out=ROWB[0:1, :], in0=bank_b[0:1, :], scalar1=inv_s,
                    scalar2=None, op0=ALU.mult))
            return vv

        def emit_ffn(idx, first_waits, v_ab_pre=None, mid_hook=None):
            if v_ab_pre is None:
                v_ab = emit_stats2(True, first_waits)
            else:
                v_ab = v_ab_pre
            if DEBUG_DUMP and idx == 0:
                dd_rows = nc.dram_tensor("dd_rows", [8, 2, 512], F32,
                                         kind="ExternalOutput").ap()
            v_pa = 0
            for s in range(NST):
                ba, bb = pb[4 + (s % 2)], pb[6 + (s % 2)]
                v_rows = make_rows(s, max(v_ab, v_pa), ba, bb)
                if DEBUG_DUMP and idx == 0:
                    ta = dma("sync", dd_rows[s, 0], ROWA[0:1, :],
                             waits=[("Vc", v_rows)])
                    tb = dma("sync", dd_rows[s, 1], ROWB[0:1, :])
                    v_rows = P.op("vector", lambda e: e.engine_nop(),
                                  waits=[ta, tb], inc="Vc")
                w0 = [("Vc", v_rows)]
                top(lambda e, ba=ba: e.matmul(
                    ba[0:128, :], ONC[0:1, :], ROWA[0:1, :],
                    start=True, stop=True), waits=w0)
                t2 = top(lambda e, bb=bb: e.matmul(
                    bb[0:128, :], ONC[0:1, :], ROWB[0:1, :],
                    start=True, stop=True))
                for c in range(NCH):
                    vop(lambda e, s=s, c=c, ba=ba: e.tensor_tensor(
                        QS[0][:, :], xt[(s, c)][:, :], ba[0:128, :], ALU.mult),
                        waits=[("Tc", t2)])
                    vop(lambda e, bb=bb: e.tensor_tensor(QS[0][:, :], QS[0][:, :],
                                                         bb[0:128, :], ALU.add))
                    vop(lambda e, c=c: e.tensor_scalar(
                        out=QS[0][:, :], in0=QS[0][:, :], scalar1=GT[:, c:c + 1],
                        scalar2=BT[:, c:c + 1], op0=ALU.mult, op1=ALU.add))
                    vop(lambda e, s=s, c=c: e.reduce_max(
                        QS[1][:, s * NCH + c:s * NCH + c + 1], QS[0][:, :], AXX,
                        apply_absolute_value=True))
                v_pa = P.val["Vc"]
            v_pa = vop(lambda e: e.reduce_max(MX[:, 0:1], QS[1][:, 0:64], AXX))
            tmx = top(lambda e: e.matmul(pb[0][0:1, 0:128], MX[:, 0:1], IDF[:, :],
                                         is_transpose=True, start=True, stop=True),
                      waits=[("Vc", v_pa)])
            vop(lambda e: e.tensor_copy(MROW[0:1, :], pb[0][0:1, 0:128]),
                waits=[("Tc", tmx)])
            vop(lambda e: e.reduce_max(SROW[0:1, 5:6], MROW[0:1, :], AXX))
            v_pay = vop(lambda e: e.tensor_copy(
                PAY[0:1, :], SROW[0:1, 5:6].broadcast_to([1, 512])))
            ds_pay = dma("sync", cc_in[idx].ap(), PAY[0:1, :], waits=[("Vc", v_pay)])
            ccv = P.op("gpsimd", lambda e: e.collective_compute(
                "AllReduce", ALU.max, replica_groups=[list(range(NCORES))],
                ins=[cc_in[idx].ap()], outs=[cc_out[idx].ap()]),
                waits=[ds_pay], inc="CC")
            ds_res = dma("sync", RES[0:1, :], cc_out[idx].ap(), waits=[("CC", ccv)])
            if mid_hook is not None:
                mid_hook()
            vop(lambda e: e.tensor_scalar(out=SROW[0:1, 6:7], in0=RES[0:1, 0:1],
                                          scalar1=1e-8, scalar2=None, op0=ALU.max),
                waits=[ds_res])
            vop(lambda e: e.memset(SROW[0:1, 7:8], QMAX))
            vop(lambda e: e.reciprocal(SROW[0:1, 7:8], SROW[0:1, 7:8]))
            vop(lambda e: e.tensor_tensor(SROW[0:1, 6:7], SROW[0:1, 6:7],
                                          SROW[0:1, 7:8], ALU.mult))
            vop(lambda e: e.reciprocal(SROW[0:1, 8:9], SROW[0:1, 6:7]))
            vop(lambda e: e.tensor_tensor(SROW[0:1, 9:10], SROW[0:1, 6:7],
                                          SROW[0:1, 4:5], ALU.mult))
            v_sc = vop(lambda e: e.tensor_scalar(out=SROW[0:1, 9:10],
                                                 in0=SROW[0:1, 9:10], scalar1=0.5,
                                                 scalar2=None, op0=ALU.mult))
            tb = top(lambda e: e.matmul(pb[1][0:128, 0:2], ONC[0:1, :],
                                        SROW[0:1, 8:10], start=True, stop=True),
                     waits=[("Vc", v_sc)])
            vop(lambda e: e.tensor_copy(SCAL[:, 1:3], pb[1][0:128, 0:2]),
                waits=[("Tc", tb)])
            vop(lambda e: e.tensor_scalar(out=GS[:, :], in0=GT[:, :],
                                          scalar1=SCAL[:, 1:2], scalar2=None,
                                          op0=ALU.mult))
            v_gs = vop(lambda e: e.tensor_scalar(out=BS[:, :], in0=BT[:, :],
                                                 scalar1=SCAL[:, 1:2], scalar2=None,
                                                 op0=ALU.mult))
            if DEBUG_DUMP and idx == 0:
                dd_srow = nc.dram_tensor("dd_srow", [1, 10], F32,
                                         kind="ExternalOutput").ap()
                tsd = dma("sync", dd_srow, SROW[0:1, 0:10], waits=[("Vc", v_gs)])
                v_gs = P.op("vector", lambda e: e.engine_nop(), waits=[tsd],
                            inc="Vc")

            bank_free = {k: 0 for k in range(4)}
            v_pb = 0
            for s in range(NST):
                ba, bb = pb[4 + (s % 2)], pb[6 + (s % 2)]
                v_rows = make_rows(s, max(v_gs, v_pb), ba, bb)
                w0 = [("Vc", v_rows)]
                top(lambda e, ba=ba: e.matmul(
                    ba[0:128, :], ONC[0:1, :], ROWA[0:1, :],
                    start=True, stop=True), waits=w0)
                t2 = top(lambda e, bb=bb: e.matmul(
                    bb[0:128, :], ONC[0:1, :], ROWB[0:1, :],
                    start=True, stop=True))
                for c in range(NCH):
                    vop(lambda e, s=s, c=c, ba=ba: e.tensor_tensor(
                        QS[0][:, :], xt[(s, c)][:, :], ba[0:128, :], ALU.mult),
                        waits=[("Tc", t2)])
                    vop(lambda e, bb=bb: e.tensor_tensor(QS[0][:, :], QS[0][:, :],
                                                         bb[0:128, :], ALU.add))
                    vop(lambda e, c=c: e.tensor_scalar(
                        out=QS[0][:, :], in0=QS[0][:, :], scalar1=GS[:, c:c + 1],
                        scalar2=BS[:, c:c + 1], op0=ALU.mult, op1=ALU.add))
                    vop(lambda e: e.tensor_scalar(
                        out=QS[0][:, :], in0=QS[0][:, :], scalar1=QMAX, scalar2=-QMAX,
                        op0=ALU.min, op1=ALU.max))
                    vop(lambda e, c=c: e.tensor_scalar(
                        out=XQ[:, c * SW:(c + 1) * SW], in0=QS[0][:, :],
                        scalar1=M32, scalar2=M32,
                        op0=ALU.add, op1=ALU.subtract))
                v_xq = P.val["Vc"]
                for oc in range(NCH):
                    bk = pb[oc % 4]
                    tw = [("Vc", max(v_xq, bank_free[oc % 4]))]
                    for c in range(NCH):
                        top(lambda e, bk=bk, c=c, oc=oc: e.matmul(
                            bk[:, :],
                            TF[:, c * 1024 + oc * 128:c * 1024 + oc * 128 + 128],
                            XQ[:, c * SW:(c + 1) * SW],
                            start=(c == 0), stop=(c == NCH - 1)), waits=tw)
                        tw = ()
                    t_h = P.val["Tc"]
                    bank_free[oc % 4] = vop(
                        lambda e, s=s, oc=oc, bk=bk: e.scalar_tensor_tensor(
                            out=xt[(s, oc)][:, :], in0=bk[:, :], scalar=SCAL[:, 2:3],
                            in1=xt[(s, oc)][:, :], op0=ALU.mult, op1=ALU.add),
                        waits=[("Tc", t_h)])
                v_pb = P.val["Vc"]
            return v_pb


        def emit_attention(first_waits):
            v_trb = emit_stats2(False, first_waits)
            spill_tokens = []
            v_xn = 0
            for st in range(NST):
                ba, bb = pb[4 + (st % 2)], pb[6 + (st % 2)]
                v_rows = make_rows(st, max(v_trb, v_xn), ba, bb)
                top(lambda e, ba=ba: e.matmul(ba[0:128, :], ONC[0:1, :], ROWA[0:1, :],
                                              start=True, stop=True),
                    waits=[("Vc", v_rows)])
                t2 = top(lambda e, bb=bb: e.matmul(bb[0:128, :], ONC[0:1, :],
                                                   ROWB[0:1, :], start=True, stop=True))
                for c in range(NCH):
                    vop(lambda e, st=st, c=c, ba=ba: e.tensor_tensor(
                        AWS[:, :], xt[(st, c)][:, :], ba[0:128, :], ALU.mult),
                        waits=[("Tc", t2), tk_tspill])
                    vv = vop(lambda e, st=st, c=c, bb=bb: e.tensor_tensor(
                        XN[c][:, st * SW:(st + 1) * SW], AWS[:, :],
                        bb[0:128, :], ALU.add))
                    spill_tokens.append(
                        dma("sync", xsp_h[st * 8 + c].rearrange("(p f) -> p f", p=128),
                            xt[(st, c)][:, :], waits=[("Vc", vv)]))
                v_xn = P.val["Vc"]
            xn_done = v_xn

            wtok = {}

            def wdma(p):
                buf = p % 2
                gates = [("twin", win_read.get(p - 2, 0))]
                wtok[p] = [
                    dma("gpsimd", WQs[buf][:, :], wq_h[:, p * 1024:(p + 1) * 1024],
                        waits=gates + spill_tokens),
                    dma("gpsimd", WKs[buf][:, :], wk_h[:, p * 1024:(p + 1) * 1024]),
                    dma("gpsimd", WVs[buf][:, :], wv_h[:, p * 1024:(p + 1) * 1024]),
                ]

            win_read = {}
            sx = {0: 0, 1: 0}
            ev = {0: 0, 1: 0}
            t_av2 = {0: 0, 1: 0}
            psidx = 0
            for p in range(NPAIR):
                if p == 0:
                    wdma(0)
                    wdma(1)
                buf = p % 2
                w_gate = [("vev", psidx)] + wtok[p] + [("vwin", P.val["vwin"])]
                if p == 0:
                    w_gate.append(("Vc", xn_done))
                for (dst, wsb, scale) in ((Qp, WQs[buf], True), (Kp, WKs[buf], False)):
                    for st in range(NST):
                        bk = pb[st % 2]
                        tw = list(w_gate) + [("vwin", P.val["vwin"] - 1)]
                        for c in range(NCH):
                            top(lambda e, bk=bk, c=c, st=st, wsb=wsb: e.matmul(
                                bk[:, :], wsb[:, c * 128:(c + 1) * 128],
                                XN[c][:, st * SW:(st + 1) * SW],
                                start=(c == 0), stop=(c == NCH - 1)), waits=tw)
                            tw = ()
                        tq = P.val["Tc"]
                        if scale:
                            P.op("vector", lambda e, st=st, bk=bk, dst=dst:
                                 e.tensor_scalar(out=dst[:, st * SW:(st + 1) * SW],
                                                 in0=bk[:, :], scalar1=0.125,
                                                 scalar2=None, op0=ALU.mult),
                                 waits=[("Tc", tq)] + spill_tokens, inc="vwin")
                        else:
                            P.op("vector", lambda e, st=st, bk=bk, dst=dst:
                                 e.tensor_copy(dst[:, st * SW:(st + 1) * SW], bk[:, :]),
                                 waits=[("Tc", tq)] + spill_tokens, inc="vwin")
                for st in range(NST):
                    bk = pb[2 + st % 2]
                    tw = list(w_gate) + [("vwin", P.val["vwin"] - 1)]
                    for c in range(NCH):
                        top(lambda e, bk=bk, c=c, st=st, wvb=WVs[buf]: e.matmul(
                            bk[:, :], wvb[:, c * 128:(c + 1) * 128],
                            XN[c][:, st * SW:(st + 1) * SW],
                            start=(c == 0), stop=(c == NCH - 1)), waits=tw)
                        tw = ()
                    tq = P.val["Tc"]
                    P.op("vector", lambda e, st=st, bk=bk:
                         e.tensor_copy(VF[:, st * SW:(st + 1) * SW], bk[:, :]),
                         waits=[("Tc", tq)] + spill_tokens, inc="vwin")
                v_vf = P.val["vwin"]
                for g in range(8):
                    bkb = pb[4 + g % 2].bitcast(BF16)
                    tw = [("vwin", v_vf)]
                    if g >= 2:
                        tw.append(("vwin", P.val["vwin"]))
                    tv = 0
                    for q in range(4):
                        tc4 = 4 * g + q
                        tv = top(lambda e, bkb=bkb, q=q, tc4=tc4: e.matmul(
                            bkb[:, q * 128:(q + 1) * 128],
                            VF[:, tc4 * 128:(tc4 + 1) * 128], IDB[:, :],
                            is_transpose=True, start=True, stop=True), waits=tw)
                        tw = ()
                    P.op("vector", lambda e, g=g, bkb=bkb:
                         e.tensor_copy(Vp[:, g * 512:(g + 1) * 512], bkb[:, 0:512]),
                         waits=[("Tc", tv)], inc="vwin")
                P.op("tensor", lambda e: e.nop(),
                     waits=[("vwin", P.val["vwin"])], inc="twin")
                win_read[p] = P.val["twin"]
                if DEBUG_DUMP and p == 0:
                    dd_q0 = nc.dram_tensor("dd_q0", [128, 4096], BF16,
                                           kind="ExternalOutput").ap()
                    dd_k0 = nc.dram_tensor("dd_k0", [128, 4096], BF16,
                                           kind="ExternalOutput").ap()
                    dd_v0 = nc.dram_tensor("dd_v0", [128, 4096], BF16,
                                           kind="ExternalOutput").ap()
                    dd_vf0 = nc.dram_tensor("dd_vf0", [128, 4096], BF16,
                                            kind="ExternalOutput").ap()
                    tq0 = dma("sync", dd_q0, Qp[:, :], waits=[("twin", win_read[0])])
                    tk0 = dma("sync", dd_k0, Kp[:, :])
                    tv0 = dma("sync", dd_v0, Vp[:, :])
                    tvf = dma("sync", dd_vf0, VF[:, :])
                    P.op("tensor", lambda e: e.nop(), waits=[tq0, tk0, tv0, tvf],
                         inc="twin")
                    win_read[p] = P.val["twin"]
                if p + 2 < NPAIR:
                    wdma(p + 2)

                for sg in range(0, NST, 2):
                    for m in range(MK + 1):
                        for j in (0, 1):
                            sxp = sx[j]
                            if m < MK:
                                st = sg + j
                                tw = [("Sc", sx[j])]
                                if m == 0 and sg == 0:
                                    tw.append(("twin", win_read[p]))
                                top(lambda e, j=j, m=m, st=st: e.matmul(
                                    sAB[j][:, 0:512],
                                    Kp[0:64, m * 128:(m + 1) * 128],
                                    Qp[0:64, st * SW:(st + 1) * SW],
                                    start=True, stop=True,
                                    tile_position=(0, 0)), waits=tw)
                                t_sc = top(lambda e, j=j, m=m, st=st: e.matmul(
                                    sAB[j][:, 512:1024],
                                    Kp[64:128, m * 128:(m + 1) * 128],
                                    Qp[64:128, st * SW:(st + 1) * SW],
                                    start=True, stop=True,
                                    tile_position=(64, 0)))
                                sx[j] = P.op("scalar", lambda e, j=j, m=m:
                                             e.activation(
                                    Pbb[j][m % 2][:, :], sAB[j][:, 0:1024],
                                    AF.Exp), waits=[("Tc", t_sc)] + spill_tokens,
                                    inc="Sc", nodrain=True)
                            if m > 0:
                                mm = m - 1
                                pbuf = Pbb[j][mm % 2]
                                tw = [("Sc", sxp)]
                                if mm == 0:
                                    tw.append(("vev", ev[j]))
                                top(lambda e, j=j, mm=mm, pbuf=pbuf: e.matmul(
                                    accR[j][0:64, :], ON128[:, 0:64],
                                    pbuf[:, 0:512], start=(mm == 0),
                                    stop=(mm == MK - 1),
                                    tile_position=(0, 0)), waits=tw)
                                top(lambda e, j=j, mm=mm, pbuf=pbuf: e.matmul(
                                    accR[j][64:128, :], ON128[:, 0:64],
                                    pbuf[:, 512:1024], start=(mm == 0),
                                    stop=(mm == MK - 1),
                                    tile_position=(0, 64)))
                                top(lambda e, j=j, mm=mm, pbuf=pbuf: e.matmul(
                                    accO[j][0:64, :],
                                    Vp[:, mm * 128:mm * 128 + 64],
                                    pbuf[:, 0:512], start=(mm == 0),
                                    stop=(mm == MK - 1),
                                    tile_position=(0, 0)))
                                t_av2[j] = top(lambda e, j=j, mm=mm, pbuf=pbuf:
                                               e.matmul(
                                    accO[j][64:128, :],
                                    Vp[:, mm * 128 + 64:mm * 128 + 128],
                                    pbuf[:, 512:1024], start=(mm == 0),
                                    stop=(mm == MK - 1),
                                    tile_position=(0, 64)))
                    for j in (0, 1):
                        st = sg + j
                        vop(lambda e, j=j: e.reciprocal(
                            RB[0:64, :], accR[j][0:64, :]),
                            waits=[("Tc", t_av2[j])] + spill_tokens)
                        vop(lambda e, p=p, st=st, j=j: e.tensor_tensor(
                            Ot[(p, st)][0:64, :], accO[j][0:64, :],
                            RB[0:64, :], ALU.mult))
                        vop(lambda e, j=j: e.reciprocal(
                            RB[64:128, :], accR[j][64:128, :]))
                        P.op("vector", lambda e, p=p, st=st, j=j: e.tensor_tensor(
                            Ot[(p, st)][64:128, :], accO[j][64:128, :],
                            RB[64:128, :], ALU.mult), inc="vev")
                        ev[j] = P.val["vev"]
                        psidx += 1

            if DEBUG_DUMP:
                dd_tra = nc.dram_tensor("dd_tra", [128, 128], F32,
                                        kind="ExternalOutput").ap()
                dd_trb = nc.dram_tensor("dd_trb", [128, 128], F32,
                                        kind="ExternalOutput").ap()
                dma("sync", dd_tra, TRA[:, 0:128], waits=[("vev", psidx)])
                dma("sync", dd_trb, TRB[:, 0:128], waits=[("vev", psidx)])
                dd_xn = nc.dram_tensor("dd_xn", [8, 128, 4096], BF16,
                                       kind="ExternalOutput").ap()
                dd_o = nc.dram_tensor("dd_o", [8, 8, 128, 512], BF16,
                                      kind="ExternalOutput").ap()
                dd_q = nc.dram_tensor("dd_q", [128, 4096], BF16,
                                      kind="ExternalOutput").ap()
                dd_k = nc.dram_tensor("dd_k", [128, 4096], BF16,
                                      kind="ExternalOutput").ap()
                dd_v = nc.dram_tensor("dd_v", [128, 4096], BF16,
                                      kind="ExternalOutput").ap()
                dbg_toks = []
                for c in range(NCH):
                    dbg_toks.append(dma("sync", dd_xn[c], XN[c][:, :],
                                        waits=[("vev", psidx)]))
                for p in range(NPAIR):
                    for st2 in range(NST):
                        dbg_toks.append(dma("sync", dd_o[p, st2],
                                            Ot[(p, st2)][:, :],
                                            waits=[("vev", psidx)]))
                dbg_toks.append(dma("sync", dd_q, Qp[:, :], waits=[("vev", psidx)]))
                dbg_toks.append(dma("sync", dd_k, Kp[:, :], waits=[("vev", psidx)]))
                dbg_toks.append(dma("sync", dd_v, Vp[:, :], waits=[("vev", psidx)]))
            wpw = [("twin", win_read[NPAIR - 1])]
            if DEBUG_DUMP:
                wpw += dbg_toks
            tk_wp = dma("gpsimd", WP[:, :], wp_h, waits=wpw)

            v_fin = 0
            pbank_free = {k: 0 for k in range(4)}
            stg_tok = {}
            for st in range(NST):
                w0 = ([("Vc", v_fin)] if st >= 2 else
                      [("twin", win_read[NPAIR - 1])])
                if DEBUG_DUMP and st < 2:
                    w0 = w0 + dbg_toks
                stg_tok[st] = [dma("sync",
                                   PSTG[st % 2][:, c * SW:(c + 1) * SW],
                                   xsp_h[st * 8 + c].rearrange("(p f) -> p f", p=128),
                                   waits=w0) for c in range(NCH)]
                for oc in range(NCH):
                    bk = pb[oc]
                    tw = [("vev", psidx), tk_wp,
                          ("Vc", max(v_fin, pbank_free.get(oc, 0)))]
                    for p in range(NPAIR):
                        top(lambda e, bk=bk, p=p, st=st, oc=oc: e.matmul(
                            bk[:, :], WP[:, p * 1024 + oc * 128:p * 1024 + oc * 128 + 128],
                            Ot[(p, st)][:, :],
                            start=(p == 0), stop=(p == NPAIR - 1)), waits=tw)
                        tw = ()
                t_strip = P.val["Tc"]
                for oc in range(NCH):
                    bk = pb[oc]
                    pbank_free[oc] = vop(
                        lambda e, st=st, oc=oc, bk=bk: e.scalar_tensor_tensor(
                            out=xt[(st, oc)][:, :], in0=bk[:, :], scalar=BP[:, oc:oc + 1],
                            in1=PSTG[st % 2][:, oc * SW:(oc + 1) * SW],
                            op0=ALU.add, op1=ALU.add),
                        waits=[("Tc", t_strip)] + stg_tok[st])
                v_fin = P.val["Vc"]
            return v_fin

        v_ab1 = stats_tail(True, [("vwin", v_round[31])])
        v_x = emit_ffn(0, [], v_ab_pre=v_ab1, mid_hook=emit_mask)

        tk_tspill = ("Vc", 0)
        if PHASES >= 2 and DEBUG_DUMP:
            dd_x1 = nc.dram_tensor("dd_x1", [8, 8, 128, 512], F32,
                                   kind="ExternalOutput").ap()
            x1toks = []
            for s_ in range(NST):
                for c_ in range(NCH):
                    x1toks.append(dma("sync", dd_x1[s_, c_], xt[(s_, c_)][:, :],
                                      waits=[("Vc", v_x)]))
            v_x = P.op("vector", lambda e: e.engine_nop(), waits=x1toks, inc="Vc")
        if PHASES >= 2:
            tk_tspill = dma("sync", tsp_h, TF[:, :], waits=[("Vc", v_x)])
            v_x = emit_attention([("Vc", v_x)])
        if PHASES >= 3:
            tk_trel = dma("sync", TF[:, :], tsp_h, waits=[("Vc", v_x)])
            v_x = emit_ffn(1, [("Vc", v_x), tk_trel])

        v_xfin = P.val["Vc"]
        ds_out = {}
        do_out = {}
        for tc in range(32):
            s, q = tc // 4, tc % 4
            buf = OSTG[tc % 2]
            tw = [("Vc", v_xfin)]
            if tc >= 2:
                tw.append(ds_out[tc - 2])
                tw.append(("DO", do_out[tc - 2]))
            slots = []
            tv = 0
            for c in range(NCH):
                bank = pb[4 * (tc % 2) + c // 4]
                slot = bank[:, (c % 4) * 128:(c % 4) * 128 + 128]
                tv = top(lambda e, o=slot,
                         i=xt[(s, c)][:, q * 128:(q + 1) * 128]:
                         e.matmul(o, i, IDF[:, :], is_transpose=True,
                                  start=True, stop=True), waits=tw)
                tw = ()
                slots.append(slot)
            for c in range(NCH):
                P.op("vector", lambda e, o=buf[:, c * 128:(c + 1) * 128], i=slots[c]:
                     e.tensor_copy(o, i), waits=[("Tc", tv)], inc="DO")
            do_out[tc] = P.val["DO"]
            ds_out[tc] = dma("sync", out_h[tc * 128:(tc + 1) * 128, :], buf[:, :],
                             waits=[("DO", do_out[tc])])
        final_waits = [(f"DS{i}", P.val[f"DS{i}"]) for i in range(8)]
        final_waits += [(f"DG{i}", P.val[f"DG{i}"]) for i in range(4)]
        P.op("sync", lambda e: e.nop(), waits=final_waits)

        @block.sync
        def _(eng):
            for f in P.streams["sync"]:
                f(eng)

        @block.gpsimd
        def _(eng):
            for f in P.streams["gpsimd"]:
                f(eng)

        @block.vector
        def _(eng):
            for f in P.streams["vector"]:
                f(eng)

        @block.scalar
        def _(eng):
            for f in P.streams["scalar"]:
                f(eng)

        @block.tensor
        def _(eng):
            for f in P.streams["tensor"]:
                f(eng)

    return nc


def host_layouts(Wq, Wk, Wv, Wp, Wf, gamma, beta, bp):
    def pack_qkv(W):
        a = np.ascontiguousarray(np.asarray(W, dtype=np.float32)).reshape(8, 2, 8, 128, 64)
        return np.ascontiguousarray(a.transpose(3, 0, 2, 1, 4).reshape(128, 8192))

    fp = np.arange(1024)
    h_, e_ = fp // 64, fp % 64
    wp_perm = np.asarray(Wp, dtype=np.float32)[e_ * 16 + h_]
    wp_h = np.ascontiguousarray(wp_perm.reshape(8, 128, 1024).transpose(1, 0, 2)
                                .reshape(128, 8192))
    wfT = np.ascontiguousarray(np.asarray(Wf, dtype=np.float32).T)
    wf_h = np.ascontiguousarray(wfT.reshape(8, 128, 1024).transpose(1, 0, 2)
                                .reshape(128, 8192))
    zo = np.zeros((128, 2), dtype=ml_dtypes.bfloat16)
    zo[:, 1] = 1
    to_t = lambda v: np.ascontiguousarray(np.asarray(v, np.float32).reshape(8, 128).T)
    return dict(
        wq=pack_qkv(Wq), wk=pack_qkv(Wk), wv=pack_qkv(Wv), wp=wp_h, wf=wf_h,
        gt=to_t(gamma), bt=to_t(beta), bpv=to_t(bp),
        idf=np.eye(128, dtype=np.float32),
        idb=np.eye(128, dtype=ml_dtypes.bfloat16),
        onc=np.ones((1, 128), dtype=np.float32),
        on128=np.ones((128, 128), dtype=ml_dtypes.bfloat16),
        zo=zo,
    )


def kernel(L0_attn_query, L0_attn_Wq, L0_attn_Wk, L0_attn_Wv, L0_attn_Wp,
           L0_attn_bp, L0_ffn_weight, L0_ffn_gamma, L0_ffn_beta):
    common = host_layouts(L0_attn_Wq, L0_attn_Wk, L0_attn_Wv, L0_attn_Wp,
                          L0_ffn_weight, L0_ffn_gamma, L0_ffn_beta, L0_attn_bp)
    x = np.ascontiguousarray(np.asarray(L0_attn_query, dtype=np.float32))
    nc = build_nc()
    in_maps = [dict(common, x=np.ascontiguousarray(x[b])) for b in range(NCORES)]
    res = run_bass_kernel_spmd(nc, in_maps, core_ids=list(range(NCORES)))
    return np.stack([np.asarray(res.results[b]["out"]) for b in range(NCORES)], axis=0)


# version 15
# speedup vs baseline: 1.0244x; 1.0244x over previous
import numpy as np
import ml_dtypes
from contextlib import ExitStack

import concourse.bass as bass
import concourse.mybir as mybir
from concourse.bass_utils import run_bass_kernel_spmd

F32 = mybir.dt.float32
BF16 = mybir.dt.bfloat16
AF = mybir.ActivationFunctionType
ALU = mybir.AluOpType
AXX = mybir.AxisListType.X

PHASES = 3
DEBUG_DUMP = False
NCORES = 8
T = 4096
D = 1024
NCH = 8
NST = 8
SW = 512
NPAIR = 8
MK = 32
EPS = 1e-5
M32 = float(3 * 2**22)
M64 = float(3 * 2**51)
QMAX = 127.0
KB = 1024


A0 = 20480


def off_x(s, c):
    return A0 + (112 - 16 * s) * KB + c * 2048


def off_O(p, s):
    return A0 + (7 - s) * 8192 + p * 1024


PA = A0 + 65536
OFF_K, OFF_V, OFF_Q = PA, PA + 8192, PA + 16384
OFF_P0, OFF_P1 = PA + 24576, PA + 26624
OFF_RB = PA + 28672
OFF_WQ, OFF_WK, OFF_WV = PA + 30720, PA + 34816, PA + 38912
OFF_VF = PA + 43008
OFF_A2B2 = PA

B0 = A0 + 131072
OFF_XN = B0
OFF_T = B0
OFF_WF = B0 + 16384
OFF_X2BF = B0 + 8192
OFF_X2BF2 = B0 + 16384
OFF_XBF2 = B0 + 24576
OFF_SUMS = B0 + 49152
OFF_QS0 = B0 + 49152
OFF_XQ = B0 + 55296
OFF_XSTG = B0 + 0
OFF_PSTG = B0
OFF_WP = B0 + 32768
OFF_OSTG = B0 + 57344

PR = A0 + 196608
OFF_IDF = PR
OFF_IDB = PR + 512
OFF_ONESC = PR + 768
OFF_ON128 = PR + 1280
OFF_ZO = PR + 1536
OFF_GT = PR + 1568
OFF_BT = PR + 1600
OFF_GS = PR + 1632
OFF_BS = PR + 1664
OFF_BP = PR + 1696
OFF_SCAL = PR + 1728
OFF_WONE = PR + 1760
OFF_MX = PR + 1792
OFF_PAY = PR + 1824
OFF_TRA = PR + 3872
OFF_TRB = PR + 4896
OFF_SROW = PR + 5920
OFF_AWS = PR + 6048
OFF_MROW = PR + 6048
OFF_ROWA = PR + 8096
OFF_RES = PR + 8096
OFF_ROWB = PR + 10144


class Prog:
    ENGINES = ("sync", "gpsimd", "vector", "scalar", "tensor")

    def __init__(self, nc):
        self.nc = nc
        self.streams = {e: [] for e in self.ENGINES}
        self.sems = {}
        self.val = {}
        self.wm = {e: {} for e in self.ENGINES}

    def mksem(self, stack, names):
        for n in names:
            self.sems[n] = stack.enter_context(self.nc.semaphore(n))
            self.val[n] = 0

    def op(self, eng, fn, waits=(), inc=None, inc_n=1, nodrain=False):
        wm = self.wm[eng]
        eff = []
        for (s, v) in waits:
            if v > 0 and wm.get(s, 0) < v:
                eff.append((s, v))
                wm[s] = v
        if inc is not None:
            self.val[inc] += inc_n
        eff = tuple(eff)
        sems = self.sems

        drain = eng in ("vector", "scalar") and not nodrain

        def run(eobj, eff=eff, fn=fn, inc=inc, inc_n=inc_n, drain=drain):
            if drain:
                eobj.drain()
            for (s, v) in eff:
                eobj.wait_ge(sems[s], v)
            last = fn(eobj)
            if inc is not None:
                last.then_inc(sems[inc], inc_n)
        self.streams[eng].append(run)
        return self.val[inc] if inc is not None else None


def build_nc():
    nc = bass.Bass("TRN2", target_bir_lowering=False,
                   detect_race_conditions=False)

    def sbt(name, off, shape, dtype):
        return nc.alloc_sbuf_tensor_at(name, shape, dtype, offset=off)

    x_h = nc.dram_tensor("x", [T, D], F32, kind="ExternalInput").ap()
    wq_h = nc.dram_tensor("wq", [128, 8192], F32, kind="ExternalInput").ap()
    wk_h = nc.dram_tensor("wk", [128, 8192], F32, kind="ExternalInput").ap()
    wv_h = nc.dram_tensor("wv", [128, 8192], F32, kind="ExternalInput").ap()
    wp_h = nc.dram_tensor("wp", [128, 8192], F32, kind="ExternalInput").ap()
    wf_h = nc.dram_tensor("wf", [128, 8192], F32, kind="ExternalInput").ap()
    gt_h = nc.dram_tensor("gt", [128, 8], F32, kind="ExternalInput").ap()
    bt_h = nc.dram_tensor("bt", [128, 8], F32, kind="ExternalInput").ap()
    bp_h = nc.dram_tensor("bpv", [128, 8], F32, kind="ExternalInput").ap()
    idf_h = nc.dram_tensor("idf", [128, 128], F32, kind="ExternalInput").ap()
    idb_h = nc.dram_tensor("idb", [128, 128], BF16, kind="ExternalInput").ap()
    onc_h = nc.dram_tensor("onc", [1, 128], F32, kind="ExternalInput").ap()
    on128_h = nc.dram_tensor("on128", [128, 128], BF16, kind="ExternalInput").ap()
    zo_h = nc.dram_tensor("zo", [128, 2], BF16, kind="ExternalInput").ap()
    out_h = nc.dram_tensor("out", [T, D], F32, kind="ExternalOutput").ap()
    xsp_h = nc.dram_tensor("xspill", [64, 65536], F32).ap()
    tsp_h = nc.dram_tensor("tspill", [128, 8192], BF16).ap()
    cc_in = [nc.dram_tensor(f"cc_in{i}", [1, 512], F32) for i in range(2)]
    cc_out = [nc.dram_tensor(f"cc_out{i}", [1, 512], F32) for i in range(2)]

    xt = {(s, c): sbt(f"x_{s}_{c}", off_x(s, c), [128, SW], F32)
          for s in range(NST) for c in range(NCH)}
    Ot = {(p, s): sbt(f"O_{p}_{s}", off_O(p, s), [128, SW], BF16)
          for p in range(NPAIR) for s in range(NST)}
    Kp = sbt("Kp", OFF_K, [128, 4096], BF16)
    Vp = sbt("Vp", OFF_V, [128, 4096], BF16)
    Qp = sbt("Qp", OFF_Q, [128, 4096], BF16)
    Pb = [sbt(f"Pb{i}", OFF_P0 + i * 2048, [128, 1024], BF16) for i in range(2)]
    Pb2 = [sbt(f"Pc{i}", PA + 51200 + i * 2048, [128, 1024], BF16)
           for i in range(2)]
    Pbb = [[Pb[0], Pb2[0]], [Pb[1], Pb2[1]]]
    RB = sbt("RB", OFF_RB, [128, SW], F32)
    WQs = [sbt(f"WQs{i}", OFF_WQ + i * 2048, [128, 1024], BF16) for i in range(2)]
    WKs = [sbt(f"WKs{i}", OFF_WK + i * 2048, [128, 1024], BF16) for i in range(2)]
    WVs = [sbt(f"WVs{i}", OFF_WV + i * 2048, [128, 1024], BF16) for i in range(2)]
    VF = sbt("VF", OFF_VF, [128, 4096], BF16)
    A2B2 = sbt("A2B2", OFF_A2B2, [2, 4096], F32)

    XN = [sbt(f"XN{c}", OFF_XN + c * 8192, [128, 4096], BF16) for c in range(NCH)]
    TF = sbt("TF", OFF_T, [128, 8192], BF16)
    WFS = sbt("WFS", OFF_WF, [128, 8192], F32)
    X2BF = sbt("X2BF", OFF_X2BF, [128, 4096], BF16)
    X2BF2 = sbt("X2BF2", OFF_X2BF2, [128, 4096], BF16)
    XBF2 = sbt("XBF2", OFF_XBF2, [128, 4096], BF16)
    SUMS = sbt("SUMS", OFF_SUMS, [2, 4096], F32)
    XQ = sbt("XQ", OFF_XQ, [128, 4096], BF16)
    QS = [sbt(f"QS{i}", OFF_QS0 + i * 2048, [128, SW], F32) for i in range(2)]
    XSTG = [sbt(f"XSTG{i}", OFF_XSTG + i * 4096, [128, 1024], F32) for i in range(2)]
    PSTG = [sbt(f"PSTG{i}", OFF_PSTG + i * 16384, [128, 4096], F32) for i in range(2)]
    WP = sbt("WP", OFF_WP, [128, 8192], BF16)
    OSTG = [sbt(f"OSTG{i}", OFF_OSTG + i * 4096, [128, 1024], F32) for i in range(2)]

    IDF = sbt("IDF", OFF_IDF, [128, 128], F32)
    IDB = sbt("IDB", OFF_IDB, [128, 128], BF16)
    ONC = sbt("ONC", OFF_ONESC, [1, 128], F32)
    ON128 = sbt("ON128", OFF_ON128, [128, 128], BF16)
    ZO = sbt("ZO", OFF_ZO, [128, 2], BF16)
    GT = sbt("GT", OFF_GT, [128, 8], F32)
    BT = sbt("BT", OFF_BT, [128, 8], F32)
    GS = sbt("GS", OFF_GS, [128, 8], F32)
    BS = sbt("BS", OFF_BS, [128, 8], F32)
    BP = sbt("BP", OFF_BP, [128, 8], F32)
    SCAL = sbt("SCAL", OFF_SCAL, [128, 8], F32)
    WONE = sbt("WONE", OFF_WONE, [128, 1], F32)
    MX = sbt("MX", OFF_MX, [128, 1], F32)
    PAY = sbt("PAY", OFF_PAY, [1, 512], F32)
    RES = sbt("RES", OFF_RES, [1, 512], F32)
    MROW = sbt("MROW", OFF_MROW, [1, 128], F32)
    TRA = sbt("TRA", OFF_TRA, [128, 256], F32)
    TRB = sbt("TRB", OFF_TRB, [128, 256], F32)
    SROW = sbt("SROW", OFF_SROW, [1, 32], F32)
    ROWA = sbt("ROWA", OFF_ROWA, [1, 512], F32)
    ROWB = sbt("ROWB", OFF_ROWB, [1, 512], F32)
    ROWA1 = sbt("ROWA1", B0 + 53248, [1, 512], F32)
    ROWB1 = sbt("ROWB1", B0 + 63488, [1, 512], F32)

    sAB = [nc.alloc_psum_tensor(f"sAB{i}", [128, 1024], F32) for i in range(2)]
    aA = nc.alloc_psum_tensor("aA", [128, 512], F32)
    aB = nc.alloc_psum_tensor("aB", [128, 512], F32)
    bA = nc.alloc_psum_tensor("bA", [128, 512], F32)
    bB = nc.alloc_psum_tensor("bB", [128, 512], F32)
    pb = [sAB[0][:, 0:512], sAB[0][:, 512:1024],
          sAB[1][:, 0:512], sAB[1][:, 512:1024],
          aA[:, :], aB[:, :], bA[:, :], bB[:, :]]
    accO = [aA, bA]
    accR = [aB, bB]

    with ExitStack() as stack:
        P = Prog(nc)
        P.mksem(stack, ["CC", "Tc", "Vc", "Sc", "DO",
                        "tsc", "tav", "sxp", "vev", "twin", "vwin"]
                + [f"DS{i}" for i in range(8)] + [f"DG{i}" for i in range(4)])
        dma_rr = {"sync": 0, "gpsimd": 0}
        dma_nch = {"sync": 8, "gpsimd": 4}
        dma_pref = {"sync": "DS", "gpsimd": "DG"}
        block = stack.enter_context(nc.Block())

        def dma(queue, out_ap, in_ap, waits=()):
            k = dma_rr[queue]
            dma_rr[queue] = (k + 1) % dma_nch[queue]
            ch = f"{dma_pref[queue]}{k}"
            gate = P.val[ch]
            w = list(waits) + ([(ch, gate)] if gate else [])
            P.op(queue, lambda e, o=out_ap, i=in_ap: e.dma_start(out=o, in_=i),
                 waits=w, inc=ch, inc_n=16)
            return (ch, P.val[ch])

        def vop(fn, waits=()):
            return P.op("vector", fn, waits=waits, inc="Vc")

        def sop(fn, waits=()):
            return P.op("scalar", fn, waits=waits, inc="Sc")

        def top(fn, waits=()):
            return P.op("tensor", fn, waits=waits, inc="Tc")

        tk_idf = dma("gpsimd", IDF[:, :], idf_h)
        consts_tokens = [tk_idf,
                         dma("gpsimd", IDB[:, :], idb_h),
                         dma("gpsimd", ONC[0:1, :], onc_h),
                         dma("gpsimd", ON128[:, :], on128_h),
                         dma("gpsimd", ZO[:, :], zo_h),
                         dma("gpsimd", GT[:, :], gt_h),
                         dma("gpsimd", BT[:, :], bt_h),
                         dma("gpsimd", BP[:, :], bp_h),
                         dma("gpsimd", WFS[:, :], wf_h)]

        vop(lambda e: e.memset(WONE[:, 0:1], 1.0))
        vop(lambda e: e.tensor_scalar(out=SCAL[:, 0:1], in0=WONE[:, 0:1], scalar1=M64,
                                      scalar2=M64, op0=ALU.add, op1=ALU.subtract))
        v_chi = P.val["Vc"]

        st_mm = {}
        sums_ev = {}

        def stats_reset():
            st_mm.clear()
            sums_ev.clear()

        def stats_strip(s, gates, x2b, xb=None):
            bk = pb[4 + (s % 2)]
            for c in range(NCH):
                tw = list(gates) if c == 0 else []
                if (s - 1, c) in st_mm:
                    tw.append(("Tc", st_mm[(s - 1, c)]))
                s1 = sop(lambda e, s=s, c=c, x2b=x2b: e.square(
                    x2b[:, c * SW:(c + 1) * SW], xt[(s, c)][:, :]), waits=tw)
                mw = [("Sc", s1)]
                if xb is None:
                    ones_mm = (lambda e, s=s, c=c, bk=bk: e.matmul(
                        bk[0:1, :], WONE[:, 0:1], xt[(s, c)][:, :],
                        start=False, stop=False))
                else:
                    v1 = vop(lambda e, s=s, c=c, xb=xb: e.tensor_copy(
                        xb[:, c * SW:(c + 1) * SW], xt[(s, c)][:, :]), waits=tw)
                    mw.append(("Vc", v1))
                    ones_mm = (lambda e, c=c, bk=bk, xb=xb: e.matmul(
                        bk[0:1, :], ON128[:, 0:1], xb[:, c * SW:(c + 1) * SW],
                        start=False, stop=False))
                if c == 0:
                    mw.append(("Vc", sums_ev.get(s - 2, 0)))
                    top(lambda e, c=c, bk=bk, x2b=x2b: e.matmul(
                        bk[0:2, :], ZO[:, :], x2b[:, c * SW:(c + 1) * SW],
                        start=True, stop=False), waits=mw)
                    st_mm[(s, c)] = top(ones_mm)
                elif c < NCH - 1:
                    top(lambda e, c=c, bk=bk, x2b=x2b: e.matmul(
                        bk[0:2, :], ZO[:, :], x2b[:, c * SW:(c + 1) * SW],
                        start=False, stop=False), waits=mw)
                    st_mm[(s, c)] = top(ones_mm)
                else:
                    top(ones_mm, waits=mw)
                    st_mm[(s, c)] = top(lambda e, c=c, bk=bk, x2b=x2b: e.matmul(
                        bk[0:2, :], ZO[:, :], x2b[:, c * SW:(c + 1) * SW],
                        start=False, stop=True))
            sums_ev[s] = vop(lambda e, s=s, bk=bk: e.tensor_copy(
                SUMS[0:2, s * SW:(s + 1) * SW], bk[0:2, :]),
                waits=[("Tc", st_mm[(s, NCH - 1)])])

        xr = x_h.rearrange("(n p) d -> n p d", p=128)
        v_round = {}
        t_round = {}
        stats_reset()
        for tc in range(32):
            buf = XSTG[tc % 2]
            qt = [dma("sync", buf[:, q2 * 256:(q2 + 1) * 256],
                      xr[tc][:, q2 * 256:(q2 + 1) * 256],
                      waits=[("Tc", t_round.get(tc - 2, 0))])
                  for q2 in range(4)]
            s, q = tc // 4, tc % 4
            slots = []
            tv = 0
            bw = [tk_idf] if tc < 2 else [tk_idf, ("vwin", v_round[tc - 2])]
            for c in range(NCH):
                bank = pb[2 * (tc % 2) + c // 4]
                slot = bank[:, (c % 4) * 128:(c % 4) * 128 + 128]
                tv = top(lambda e, o=slot, i=buf[:, c * 128:(c + 1) * 128]:
                         e.matmul(o, i, IDF[:, :], is_transpose=True,
                                  start=True, stop=True),
                         waits=[qt[c // 2]] + bw)
                bw = []
                slots.append(slot)
            t_round[tc] = tv
            last_v = None
            for c in range(NCH):
                last_v = P.op("vector",
                              lambda e, o=xt[(s, c)][:, q * 128:(q + 1) * 128],
                              i=slots[c]:
                              e.tensor_copy(o, i), waits=[("Tc", tv)], inc="vwin")
            v_round[tc] = last_v
            if q == 3:
                stats_strip(s, consts_tokens + [("vwin", v_round[tc])], X2BF)
        x_loaded = P.val["vwin"]
        t_xload = P.val["Tc"]

        AWS = sbt("AWS", OFF_AWS, [128, 512], F32)
        AWS2 = sbt("AWS2", OFF_XQ, [128, 512], F32)

        def emit_mask():
            v0 = P.val["Vc"]
            t_saw = 0
            for k in range(16):
                sa = sop(lambda e, k=k: e.activation(
                    AWS[:, :], WFS[:, k * 512:(k + 1) * 512], AF.Abs),
                    waits=consts_tokens + [("Vc", max(v_chi, v0)),
                                           ("Tc", t_saw)])
                t_saw = top(lambda e, k=k: e.matmul(
                    pb[4][0:1, :], WONE[:, 0:1], AWS[:, :],
                    start=(k == 0), stop=(k == 15)), waits=[("Sc", sa)])
            t_sumaw = P.val["Tc"]
            vop(lambda e: e.reduce_sum(SROW[0:1, 0:1], pb[4][0:1, :], AXX),
                waits=[("Tc", t_sumaw)])
            v_d = vop(lambda e: e.tensor_scalar(
                out=SROW[0:1, 1:2], in0=SROW[0:1, 0:1],
                scalar1=float(2.0**-20), scalar2=0.7,
                op0=ALU.mult, op1=ALU.mult))
            tv = top(lambda e: e.matmul(pb[5][0:128, 0:1], ONC[0:1, :],
                                        SROW[0:1, 1:2], start=True, stop=True),
                     waits=[("Vc", v_d)])
            v_db = vop(lambda e: e.tensor_copy(SCAL[:, 3:4], pb[5][0:128, 0:1]),
                       waits=[("Tc", tv)])
            v_tf = v_db
            t_cnt = 0
            for k in range(16):
                sa = sop(lambda e, k=k: e.activation(
                    AWS[:, :], WFS[:, k * 512:(k + 1) * 512], AF.Abs),
                    waits=[("Vc", v_tf)])
                va = vop(lambda e: e.scalar_tensor_tensor(
                    out=AWS2[:, :], in0=AWS[:, :], scalar=SCAL[:, 3:4],
                    in1=AWS[:, :], op0=ALU.is_gt, op1=ALU.mult),
                    waits=[("Sc", sa), ("Tc", t_cnt)])
                top(lambda e, k=k: e.matmul(pb[7][0:1, :], WONE[:, 0:1],
                                            AWS2[:, :],
                                            start=(k == 0), stop=(k == 15)),
                    waits=[("Vc", va)])
                vb = vop(lambda e: e.tensor_scalar(
                    out=AWS2[:, :], in0=AWS[:, :], scalar1=SCAL[:, 3:4],
                    scalar2=None, op0=ALU.is_gt),
                    waits=[("Tc", P.val["Tc"])])
                t_cnt = top(lambda e, k=k: e.matmul(
                    pb[6][0:1, :], WONE[:, 0:1], AWS2[:, :],
                    start=(k == 0), stop=(k == 15)), waits=[("Vc", vb)])
                sg = sop(lambda e, k=k: e.activation(
                    AWS[:, :], WFS[:, k * 512:(k + 1) * 512], AF.Sign),
                    waits=[("Vc", vb)])
                v_tf = vop(lambda e, k=k: e.tensor_tensor(
                    TF[:, k * 512:(k + 1) * 512], AWS[:, :], AWS2[:, :],
                    ALU.mult),
                    waits=[("Sc", sg), ("Tc", max(t_cnt, t_xload))])
            t_maw = P.val["Tc"]
            vop(lambda e: e.reduce_sum(SROW[0:1, 2:3], pb[6][0:1, :], AXX),
                waits=[("Tc", t_maw)])
            vop(lambda e: e.reduce_sum(SROW[0:1, 3:4], pb[7][0:1, :], AXX))
            vop(lambda e: e.tensor_scalar(out=SROW[0:1, 2:3], in0=SROW[0:1, 2:3],
                                          scalar1=1.0, scalar2=None, op0=ALU.max))
            vop(lambda e: e.reciprocal(SROW[0:1, 2:3], SROW[0:1, 2:3]))
            vop(lambda e: e.tensor_tensor(SROW[0:1, 4:5], SROW[0:1, 3:4],
                                          SROW[0:1, 2:3], ALU.mult))

        def ap2(tensor, off_elems, step, count):
            return bass.AP(tensor, off_elems, [[256, 128], [step, count]])

        def emit_stats2(need_r2, first_waits):
            stats_reset()
            fw = list(first_waits)
            for s in range(NST):
                stats_strip(s, fw, X2BF2, xb=XBF2)
                fw = []
            return stats_tail(need_r2)

        def stats_tail(need_r2, extra_tw=()):
            tw = [("Vc", sums_ev[NST - 1])] + list(extra_tw)
            for j in range(32):
                top(lambda e, j=j: e.matmul(pb[0][:, 2 * j:2 * j + 2],
                                            SUMS[0:2, j * 128:(j + 1) * 128],
                                            IDF[0:2, 0:2], is_transpose=True,
                                            start=True, stop=True), waits=tw)
                tw = ()
            t_tr = P.val["Tc"]
            vop(lambda e: e.tensor_copy(TRA[:, 0:64], pb[0][:, 0:64]),
                waits=[("Tc", t_tr)])

            s1 = ap2(TRA, 0, 2, 32)
            s2 = ap2(TRA, 1, 2, 32)
            mu = ap2(TRA, 64, 2, 32)
            var = ap2(TRA, 65, 2, 32)
            r2ap = ap2(TRA, 128, 2, 32)
            av = ap2(TRB, 0, 2, 32)
            bv = ap2(TRB, 1, 2, 32)
            r0 = ap2(TRB, 64, 2, 32)
            tt = ap2(TRB, 65, 2, 32)
            srcbuf = ap2(TRB, 128, 2, 32)

            def rsqrt_into(dst):
                vr = vop(lambda e: e.reciprocal(r0, srcbuf))
                sv = sop(lambda e: e.activation(dst, r0, AF.Sqrt), waits=[("Vc", vr)])
                for it in range(2):
                    vop(lambda e: e.tensor_tensor(tt, dst, dst, ALU.mult),
                        waits=[("Sc", sv)])
                    vop(lambda e: e.tensor_tensor(tt, tt, srcbuf, ALU.mult))
                    vop(lambda e: e.tensor_scalar(out=tt, in0=tt, scalar1=-0.5,
                                                  scalar2=1.5, op0=ALU.mult,
                                                  op1=ALU.add))
                    vop(lambda e: e.tensor_tensor(dst, dst, tt, ALU.mult))

            vop(lambda e: e.tensor_scalar(out=mu, in0=s1, scalar1=float(1.0 / D),
                                          scalar2=None, op0=ALU.mult))
            vop(lambda e: e.tensor_scalar(out=var, in0=s2, scalar1=float(1.0 / D),
                                          scalar2=None, op0=ALU.mult))
            vop(lambda e: e.tensor_tensor(tt, mu, mu, ALU.mult))
            vop(lambda e: e.tensor_tensor(var, var, tt, ALU.subtract))
            vop(lambda e: e.tensor_scalar(out=srcbuf, in0=var, scalar1=EPS,
                                          scalar2=None, op0=ALU.add))
            rsqrt_into(av)
            if need_r2:
                vop(lambda e: e.tensor_tensor(tt, av, av, ALU.mult))
                vop(lambda e: e.tensor_tensor(tt, tt, var, ALU.mult))
                vop(lambda e: e.tensor_scalar(out=srcbuf, in0=tt, scalar1=EPS,
                                              scalar2=None, op0=ALU.add))
                rsqrt_into(r2ap)
                vop(lambda e: e.tensor_tensor(av, av, r2ap, ALU.mult))
            vop(lambda e: e.tensor_tensor(bv, mu, av, ALU.mult))
            v_ab0 = vop(lambda e: e.tensor_scalar(out=bv, in0=bv, scalar1=-1.0,
                                                  scalar2=None, op0=ALU.mult))
            return v_ab0

        def make_rows(srow, v_trb, bank_a, bank_b, extra_waits=(), inv_s=None,
                      rows=None):
            tw = [("Vc", v_trb)] + list(extra_waits)
            for jj in range(4):
                j = 4 * srow + jj
                top(lambda e, j=j, jj=jj, bank_a=bank_a: e.matmul(
                    bank_a[0:1, jj * 128:(jj + 1) * 128],
                    TRB[:, 2 * j:2 * j + 1], IDF[:, :], is_transpose=True,
                    start=True, stop=True), waits=tw)
                tw = ()
                top(lambda e, j=j, jj=jj, bank_b=bank_b: e.matmul(
                    bank_b[0:1, jj * 128:(jj + 1) * 128],
                    TRB[:, 2 * j + 1:2 * j + 2], IDF[:, :], is_transpose=True,
                    start=True, stop=True))
            t_bk = P.val["Tc"]
            ra, rb = rows if rows is not None else (ROWA, ROWB)
            if inv_s is None:
                vop(lambda e, bank_a=bank_a, ra=ra: e.tensor_copy(
                    ra[0:1, :], bank_a[0:1, :]), waits=[("Tc", t_bk)])
                vv = vop(lambda e, bank_b=bank_b, rb=rb: e.tensor_copy(
                    rb[0:1, :], bank_b[0:1, :]))
            else:
                vop(lambda e, bank_a=bank_a, ra=ra: e.tensor_scalar(
                    out=ra[0:1, :], in0=bank_a[0:1, :], scalar1=inv_s,
                    scalar2=None, op0=ALU.mult), waits=[("Tc", t_bk)])
                vv = vop(lambda e, bank_b=bank_b, rb=rb: e.tensor_scalar(
                    out=rb[0:1, :], in0=bank_b[0:1, :], scalar1=inv_s,
                    scalar2=None, op0=ALU.mult))
            return vv

        def emit_ffn(idx, first_waits, v_ab_pre=None, mid_hook=None):
            if v_ab_pre is None:
                v_ab = emit_stats2(True, first_waits)
            else:
                v_ab = v_ab_pre
            if DEBUG_DUMP and idx == 0:
                dd_rows = nc.dram_tensor("dd_rows", [8, 2, 512], F32,
                                         kind="ExternalOutput").ap()
            def rows_bcast(s, base_v):
                ba, bb = pb[4 + (s % 2)], pb[6 + (s % 2)]
                rp = (ROWA, ROWB) if s % 2 == 0 else (ROWA1, ROWB1)
                v_rows = make_rows(s, base_v, ba, bb, rows=rp)
                top(lambda e, ba=ba, rp=rp: e.matmul(
                    ba[0:128, :], ONC[0:1, :], rp[0][0:1, :],
                    start=True, stop=True), waits=[("Vc", v_rows)])
                return top(lambda e, bb=bb, rp=rp: e.matmul(
                    bb[0:128, :], ONC[0:1, :], rp[1][0:1, :],
                    start=True, stop=True))

            chv = {}
            t2s = {0: rows_bcast(0, v_ab)}
            for s in range(NST):
                ba, bb = pb[4 + (s % 2)], pb[6 + (s % 2)]
                if s + 1 < NST:
                    t2s[s + 1] = rows_bcast(s + 1, max(v_ab, chv.get(s - 1, 0)))
                for c in range(NCH):
                    vop(lambda e, s=s, c=c, ba=ba: e.tensor_tensor(
                        QS[0][:, :], xt[(s, c)][:, :], ba[0:128, :], ALU.mult),
                        waits=[("Tc", t2s[s])])
                    vop(lambda e, bb=bb: e.tensor_tensor(QS[0][:, :], QS[0][:, :],
                                                         bb[0:128, :], ALU.add))
                    vop(lambda e, c=c: e.tensor_scalar(
                        out=QS[0][:, :], in0=QS[0][:, :], scalar1=GT[:, c:c + 1],
                        scalar2=BT[:, c:c + 1], op0=ALU.mult, op1=ALU.add))
                    vop(lambda e, s=s, c=c: e.reduce_max(
                        QS[1][:, s * NCH + c:s * NCH + c + 1], QS[0][:, :], AXX,
                        apply_absolute_value=True))
                chv[s] = P.val["Vc"]
            v_pa = P.val["Vc"]
            v_pa = vop(lambda e: e.reduce_max(MX[:, 0:1], QS[1][:, 0:64], AXX))
            tmx = top(lambda e: e.matmul(pb[0][0:1, 0:128], MX[:, 0:1], IDF[:, :],
                                         is_transpose=True, start=True, stop=True),
                      waits=[("Vc", v_pa)])
            vop(lambda e: e.tensor_copy(MROW[0:1, :], pb[0][0:1, 0:128]),
                waits=[("Tc", tmx)])
            vop(lambda e: e.reduce_max(SROW[0:1, 5:6], MROW[0:1, :], AXX))
            v_pay = vop(lambda e: e.tensor_copy(
                PAY[0:1, :], SROW[0:1, 5:6].broadcast_to([1, 512])))
            ds_pay = dma("sync", cc_in[idx].ap(), PAY[0:1, :], waits=[("Vc", v_pay)])
            ccv = P.op("gpsimd", lambda e: e.collective_compute(
                "AllReduce", ALU.max, replica_groups=[list(range(NCORES))],
                ins=[cc_in[idx].ap()], outs=[cc_out[idx].ap()]),
                waits=[ds_pay], inc="CC")
            ds_res = dma("sync", RES[0:1, :], cc_out[idx].ap(), waits=[("CC", ccv)])
            if mid_hook is not None:
                mid_hook()
            vop(lambda e: e.tensor_scalar(out=SROW[0:1, 6:7], in0=RES[0:1, 0:1],
                                          scalar1=1e-8, scalar2=None, op0=ALU.max),
                waits=[ds_res])
            vop(lambda e: e.memset(SROW[0:1, 7:8], QMAX))
            vop(lambda e: e.reciprocal(SROW[0:1, 7:8], SROW[0:1, 7:8]))
            vop(lambda e: e.tensor_tensor(SROW[0:1, 6:7], SROW[0:1, 6:7],
                                          SROW[0:1, 7:8], ALU.mult))
            vop(lambda e: e.reciprocal(SROW[0:1, 8:9], SROW[0:1, 6:7]))
            vop(lambda e: e.tensor_tensor(SROW[0:1, 9:10], SROW[0:1, 6:7],
                                          SROW[0:1, 4:5], ALU.mult))
            v_sc = vop(lambda e: e.tensor_scalar(out=SROW[0:1, 9:10],
                                                 in0=SROW[0:1, 9:10], scalar1=0.5,
                                                 scalar2=None, op0=ALU.mult))
            tb = top(lambda e: e.matmul(pb[1][0:128, 0:2], ONC[0:1, :],
                                        SROW[0:1, 8:10], start=True, stop=True),
                     waits=[("Vc", v_sc)])
            vop(lambda e: e.tensor_copy(SCAL[:, 1:3], pb[1][0:128, 0:2]),
                waits=[("Tc", tb)])
            vop(lambda e: e.tensor_scalar(out=GS[:, :], in0=GT[:, :],
                                          scalar1=SCAL[:, 1:2], scalar2=None,
                                          op0=ALU.mult))
            v_gs = vop(lambda e: e.tensor_scalar(out=BS[:, :], in0=BT[:, :],
                                                 scalar1=SCAL[:, 1:2], scalar2=None,
                                                 op0=ALU.mult))
            if DEBUG_DUMP and idx == 0:
                dd_srow = nc.dram_tensor("dd_srow", [1, 10], F32,
                                         kind="ExternalOutput").ap()
                tsd = dma("sync", dd_srow, SROW[0:1, 0:10], waits=[("Vc", v_gs)])
                v_gs = P.op("vector", lambda e: e.engine_nop(), waits=[tsd],
                            inc="Vc")

            bank_free = {k: 0 for k in range(4)}
            chv = {}
            t2s = {0: rows_bcast(0, v_gs)}
            for s in range(NST):
                ba, bb = pb[4 + (s % 2)], pb[6 + (s % 2)]
                if s + 1 < NST:
                    t2s[s + 1] = rows_bcast(s + 1, max(v_gs, chv.get(s - 1, 0)))
                for c in range(NCH):
                    vop(lambda e, s=s, c=c, ba=ba: e.tensor_tensor(
                        QS[0][:, :], xt[(s, c)][:, :], ba[0:128, :], ALU.mult),
                        waits=[("Tc", t2s[s])])
                    vop(lambda e, bb=bb: e.tensor_tensor(QS[0][:, :], QS[0][:, :],
                                                         bb[0:128, :], ALU.add))
                    vop(lambda e, c=c: e.tensor_scalar(
                        out=QS[0][:, :], in0=QS[0][:, :], scalar1=GS[:, c:c + 1],
                        scalar2=BS[:, c:c + 1], op0=ALU.mult, op1=ALU.add))
                    vop(lambda e: e.tensor_scalar(
                        out=QS[0][:, :], in0=QS[0][:, :], scalar1=QMAX, scalar2=-QMAX,
                        op0=ALU.min, op1=ALU.max))
                    vop(lambda e, c=c: e.tensor_scalar(
                        out=XQ[:, c * SW:(c + 1) * SW], in0=QS[0][:, :],
                        scalar1=M32, scalar2=M32,
                        op0=ALU.add, op1=ALU.subtract))
                v_xq = P.val["Vc"]
                for oc in range(NCH):
                    bk = pb[oc % 4]
                    tw = [("Vc", max(v_xq, bank_free[oc % 4]))]
                    for c in range(NCH):
                        top(lambda e, bk=bk, c=c, oc=oc: e.matmul(
                            bk[:, :],
                            TF[:, c * 1024 + oc * 128:c * 1024 + oc * 128 + 128],
                            XQ[:, c * SW:(c + 1) * SW],
                            start=(c == 0), stop=(c == NCH - 1)), waits=tw)
                        tw = ()
                    t_h = P.val["Tc"]
                    bank_free[oc % 4] = vop(
                        lambda e, s=s, oc=oc, bk=bk: e.scalar_tensor_tensor(
                            out=xt[(s, oc)][:, :], in0=bk[:, :], scalar=SCAL[:, 2:3],
                            in1=xt[(s, oc)][:, :], op0=ALU.mult, op1=ALU.add),
                        waits=[("Tc", t_h)])
                chv[s] = P.val["Vc"]
            return P.val["Vc"]


        def emit_attention(first_waits):
            v_trb = emit_stats2(False, first_waits)
            spill_tokens = []
            v_xn = 0
            for st in range(NST):
                ba, bb = pb[4 + (st % 2)], pb[6 + (st % 2)]
                v_rows = make_rows(st, max(v_trb, v_xn), ba, bb)
                top(lambda e, ba=ba: e.matmul(ba[0:128, :], ONC[0:1, :], ROWA[0:1, :],
                                              start=True, stop=True),
                    waits=[("Vc", v_rows)])
                t2 = top(lambda e, bb=bb: e.matmul(bb[0:128, :], ONC[0:1, :],
                                                   ROWB[0:1, :], start=True, stop=True))
                for c in range(NCH):
                    vop(lambda e, st=st, c=c, ba=ba: e.tensor_tensor(
                        AWS[:, :], xt[(st, c)][:, :], ba[0:128, :], ALU.mult),
                        waits=[("Tc", t2), tk_tspill])
                    vv = vop(lambda e, st=st, c=c, bb=bb: e.tensor_tensor(
                        XN[c][:, st * SW:(st + 1) * SW], AWS[:, :],
                        bb[0:128, :], ALU.add))
                    spill_tokens.append(
                        dma("sync", xsp_h[st * 8 + c].rearrange("(p f) -> p f", p=128),
                            xt[(st, c)][:, :], waits=[("Vc", vv)]))
                v_xn = P.val["Vc"]
            xn_done = v_xn

            wtok = {}

            def wdma(p):
                buf = p % 2
                gates = [("twin", win_read.get(p - 2, 0))]
                wtok[p] = [
                    dma("gpsimd", WQs[buf][:, :], wq_h[:, p * 1024:(p + 1) * 1024],
                        waits=gates + spill_tokens),
                    dma("gpsimd", WKs[buf][:, :], wk_h[:, p * 1024:(p + 1) * 1024]),
                    dma("gpsimd", WVs[buf][:, :], wv_h[:, p * 1024:(p + 1) * 1024]),
                ]

            win_read = {}
            sx = {0: 0, 1: 0}
            ev = {0: 0, 1: 0}
            t_av2 = {0: 0, 1: 0}
            psidx = 0
            for p in range(NPAIR):
                if p == 0:
                    wdma(0)
                    wdma(1)
                buf = p % 2
                w_gate = [("vev", psidx)] + wtok[p] + [("vwin", P.val["vwin"])]
                if p == 0:
                    w_gate.append(("Vc", xn_done))
                for (dst, wsb, scale) in ((Qp, WQs[buf], True), (Kp, WKs[buf], False)):
                    for st in range(NST):
                        bk = pb[st % 2]
                        tw = list(w_gate) + [("vwin", P.val["vwin"] - 1)]
                        for c in range(NCH):
                            top(lambda e, bk=bk, c=c, st=st, wsb=wsb: e.matmul(
                                bk[:, :], wsb[:, c * 128:(c + 1) * 128],
                                XN[c][:, st * SW:(st + 1) * SW],
                                start=(c == 0), stop=(c == NCH - 1)), waits=tw)
                            tw = ()
                        tq = P.val["Tc"]
                        if scale:
                            P.op("vector", lambda e, st=st, bk=bk, dst=dst:
                                 e.tensor_scalar(out=dst[:, st * SW:(st + 1) * SW],
                                                 in0=bk[:, :], scalar1=0.125,
                                                 scalar2=None, op0=ALU.mult),
                                 waits=[("Tc", tq)] + spill_tokens, inc="vwin")
                        else:
                            P.op("vector", lambda e, st=st, bk=bk, dst=dst:
                                 e.tensor_copy(dst[:, st * SW:(st + 1) * SW], bk[:, :]),
                                 waits=[("Tc", tq)] + spill_tokens, inc="vwin")
                for st in range(NST):
                    bk = pb[2 + st % 2]
                    tw = list(w_gate) + [("vwin", P.val["vwin"] - 1)]
                    for c in range(NCH):
                        top(lambda e, bk=bk, c=c, st=st, wvb=WVs[buf]: e.matmul(
                            bk[:, :], wvb[:, c * 128:(c + 1) * 128],
                            XN[c][:, st * SW:(st + 1) * SW],
                            start=(c == 0), stop=(c == NCH - 1)), waits=tw)
                        tw = ()
                    tq = P.val["Tc"]
                    P.op("vector", lambda e, st=st, bk=bk:
                         e.tensor_copy(VF[:, st * SW:(st + 1) * SW], bk[:, :]),
                         waits=[("Tc", tq)] + spill_tokens, inc="vwin")
                v_vf = P.val["vwin"]
                for g in range(8):
                    bkb = pb[4 + g % 2].bitcast(BF16)
                    tw = [("vwin", v_vf)]
                    if g >= 2:
                        tw.append(("vwin", P.val["vwin"]))
                    tv = 0
                    for q in range(4):
                        tc4 = 4 * g + q
                        tv = top(lambda e, bkb=bkb, q=q, tc4=tc4: e.matmul(
                            bkb[:, q * 128:(q + 1) * 128],
                            VF[:, tc4 * 128:(tc4 + 1) * 128], IDB[:, :],
                            is_transpose=True, start=True, stop=True), waits=tw)
                        tw = ()
                    P.op("vector", lambda e, g=g, bkb=bkb:
                         e.tensor_copy(Vp[:, g * 512:(g + 1) * 512], bkb[:, 0:512]),
                         waits=[("Tc", tv)], inc="vwin")
                P.op("tensor", lambda e: e.nop(),
                     waits=[("vwin", P.val["vwin"])], inc="twin")
                win_read[p] = P.val["twin"]
                if DEBUG_DUMP and p == 0:
                    dd_q0 = nc.dram_tensor("dd_q0", [128, 4096], BF16,
                                           kind="ExternalOutput").ap()
                    dd_k0 = nc.dram_tensor("dd_k0", [128, 4096], BF16,
                                           kind="ExternalOutput").ap()
                    dd_v0 = nc.dram_tensor("dd_v0", [128, 4096], BF16,
                                           kind="ExternalOutput").ap()
                    dd_vf0 = nc.dram_tensor("dd_vf0", [128, 4096], BF16,
                                            kind="ExternalOutput").ap()
                    tq0 = dma("sync", dd_q0, Qp[:, :], waits=[("twin", win_read[0])])
                    tk0 = dma("sync", dd_k0, Kp[:, :])
                    tv0 = dma("sync", dd_v0, Vp[:, :])
                    tvf = dma("sync", dd_vf0, VF[:, :])
                    P.op("tensor", lambda e: e.nop(), waits=[tq0, tk0, tv0, tvf],
                         inc="twin")
                    win_read[p] = P.val["twin"]
                if p + 2 < NPAIR:
                    wdma(p + 2)

                for sg in range(0, NST, 2):
                    for m in range(MK + 1):
                        for j in (0, 1):
                            sxp = sx[j]
                            if m < MK:
                                st = sg + j
                                tw = [("Sc", sx[j])]
                                if m == 0 and sg == 0:
                                    tw.append(("twin", win_read[p]))
                                top(lambda e, j=j, m=m, st=st: e.matmul(
                                    sAB[j][:, 0:512],
                                    Kp[0:64, m * 128:(m + 1) * 128],
                                    Qp[0:64, st * SW:(st + 1) * SW],
                                    start=True, stop=True,
                                    tile_position=(0, 0)), waits=tw)
                                t_sc = top(lambda e, j=j, m=m, st=st: e.matmul(
                                    sAB[j][:, 512:1024],
                                    Kp[64:128, m * 128:(m + 1) * 128],
                                    Qp[64:128, st * SW:(st + 1) * SW],
                                    start=True, stop=True,
                                    tile_position=(64, 0)))
                                sx[j] = P.op("scalar", lambda e, j=j, m=m:
                                             e.activation(
                                    Pbb[j][m % 2][:, :], sAB[j][:, 0:1024],
                                    AF.Exp), waits=[("Tc", t_sc)] + spill_tokens,
                                    inc="Sc", nodrain=True)
                            if m > 0:
                                mm = m - 1
                                pbuf = Pbb[j][mm % 2]
                                tw = [("Sc", sxp)]
                                if mm == 0:
                                    tw.append(("vev", ev[j]))
                                top(lambda e, j=j, mm=mm, pbuf=pbuf: e.matmul(
                                    accO[j][0:64, :],
                                    Vp[:, mm * 128:mm * 128 + 64],
                                    pbuf[:, 0:512], start=(mm == 0),
                                    stop=(mm == MK - 1),
                                    tile_position=(0, 0)), waits=tw)
                                top(lambda e, j=j, mm=mm, pbuf=pbuf: e.matmul(
                                    accO[j][64:128, :],
                                    Vp[:, mm * 128 + 64:mm * 128 + 128],
                                    pbuf[:, 512:1024], start=(mm == 0),
                                    stop=(mm == MK - 1),
                                    tile_position=(0, 64)))
                                top(lambda e, j=j, mm=mm, pbuf=pbuf: e.matmul(
                                    accR[j][0:64, :], ON128[:, 0:64],
                                    pbuf[:, 0:512], start=(mm == 0),
                                    stop=(mm == MK - 1),
                                    tile_position=(0, 0)))
                                t_av2[j] = top(lambda e, j=j, mm=mm, pbuf=pbuf:
                                               e.matmul(
                                    accR[j][64:128, :], ON128[:, 0:64],
                                    pbuf[:, 512:1024], start=(mm == 0),
                                    stop=(mm == MK - 1),
                                    tile_position=(0, 64)))
                    for j in (0, 1):
                        st = sg + j
                        vop(lambda e, j=j: e.reciprocal(
                            RB[0:64, :], accR[j][0:64, :]),
                            waits=[("Tc", t_av2[j])] + spill_tokens)
                        vop(lambda e, p=p, st=st, j=j: e.tensor_tensor(
                            Ot[(p, st)][0:64, :], accO[j][0:64, :],
                            RB[0:64, :], ALU.mult))
                        vop(lambda e, j=j: e.reciprocal(
                            RB[64:128, :], accR[j][64:128, :]))
                        P.op("vector", lambda e, p=p, st=st, j=j: e.tensor_tensor(
                            Ot[(p, st)][64:128, :], accO[j][64:128, :],
                            RB[64:128, :], ALU.mult), inc="vev")
                        ev[j] = P.val["vev"]
                        psidx += 1

            if DEBUG_DUMP:
                dd_tra = nc.dram_tensor("dd_tra", [128, 128], F32,
                                        kind="ExternalOutput").ap()
                dd_trb = nc.dram_tensor("dd_trb", [128, 128], F32,
                                        kind="ExternalOutput").ap()
                dma("sync", dd_tra, TRA[:, 0:128], waits=[("vev", psidx)])
                dma("sync", dd_trb, TRB[:, 0:128], waits=[("vev", psidx)])
                dd_xn = nc.dram_tensor("dd_xn", [8, 128, 4096], BF16,
                                       kind="ExternalOutput").ap()
                dd_o = nc.dram_tensor("dd_o", [8, 8, 128, 512], BF16,
                                      kind="ExternalOutput").ap()
                dd_q = nc.dram_tensor("dd_q", [128, 4096], BF16,
                                      kind="ExternalOutput").ap()
                dd_k = nc.dram_tensor("dd_k", [128, 4096], BF16,
                                      kind="ExternalOutput").ap()
                dd_v = nc.dram_tensor("dd_v", [128, 4096], BF16,
                                      kind="ExternalOutput").ap()
                dbg_toks = []
                for c in range(NCH):
                    dbg_toks.append(dma("sync", dd_xn[c], XN[c][:, :],
                                        waits=[("vev", psidx)]))
                for p in range(NPAIR):
                    for st2 in range(NST):
                        dbg_toks.append(dma("sync", dd_o[p, st2],
                                            Ot[(p, st2)][:, :],
                                            waits=[("vev", psidx)]))
                dbg_toks.append(dma("sync", dd_q, Qp[:, :], waits=[("vev", psidx)]))
                dbg_toks.append(dma("sync", dd_k, Kp[:, :], waits=[("vev", psidx)]))
                dbg_toks.append(dma("sync", dd_v, Vp[:, :], waits=[("vev", psidx)]))
            wpw = [("twin", win_read[NPAIR - 1])]
            if DEBUG_DUMP:
                wpw += dbg_toks
            tk_wp = dma("gpsimd", WP[:, :], wp_h, waits=wpw)

            v_fin = 0
            pbank_free = {k: 0 for k in range(4)}
            stg_tok = {}
            for st in range(NST):
                w0 = ([("Vc", v_fin)] if st >= 2 else
                      [("twin", win_read[NPAIR - 1])])
                if DEBUG_DUMP and st < 2:
                    w0 = w0 + dbg_toks
                stg_tok[st] = [dma("sync",
                                   PSTG[st % 2][:, c * SW:(c + 1) * SW],
                                   xsp_h[st * 8 + c].rearrange("(p f) -> p f", p=128),
                                   waits=w0) for c in range(NCH)]
                for oc in range(NCH):
                    bk = pb[oc]
                    tw = [("vev", psidx), tk_wp,
                          ("Vc", max(v_fin, pbank_free.get(oc, 0)))]
                    for p in range(NPAIR):
                        top(lambda e, bk=bk, p=p, st=st, oc=oc: e.matmul(
                            bk[:, :], WP[:, p * 1024 + oc * 128:p * 1024 + oc * 128 + 128],
                            Ot[(p, st)][:, :],
                            start=(p == 0), stop=(p == NPAIR - 1)), waits=tw)
                        tw = ()
                t_strip = P.val["Tc"]
                for oc in range(NCH):
                    bk = pb[oc]
                    pbank_free[oc] = vop(
                        lambda e, st=st, oc=oc, bk=bk: e.scalar_tensor_tensor(
                            out=xt[(st, oc)][:, :], in0=bk[:, :], scalar=BP[:, oc:oc + 1],
                            in1=PSTG[st % 2][:, oc * SW:(oc + 1) * SW],
                            op0=ALU.add, op1=ALU.add),
                        waits=[("Tc", t_strip)] + stg_tok[st])
                v_fin = P.val["Vc"]
            return v_fin

        v_ab1 = stats_tail(True, [("vwin", v_round[31])])
        v_x = emit_ffn(0, [], v_ab_pre=v_ab1, mid_hook=emit_mask)

        tk_tspill = ("Vc", 0)
        if PHASES >= 2 and DEBUG_DUMP:
            dd_x1 = nc.dram_tensor("dd_x1", [8, 8, 128, 512], F32,
                                   kind="ExternalOutput").ap()
            x1toks = []
            for s_ in range(NST):
                for c_ in range(NCH):
                    x1toks.append(dma("sync", dd_x1[s_, c_], xt[(s_, c_)][:, :],
                                      waits=[("Vc", v_x)]))
            v_x = P.op("vector", lambda e: e.engine_nop(), waits=x1toks, inc="Vc")
        if PHASES >= 2:
            tk_tspill = dma("sync", tsp_h, TF[:, :], waits=[("Vc", v_x)])
            v_x = emit_attention([("Vc", v_x)])
        if PHASES >= 3:
            tk_trel = dma("sync", TF[:, :], tsp_h, waits=[("Vc", v_x)])
            v_x = emit_ffn(1, [("Vc", v_x), tk_trel])

        v_xfin = P.val["Vc"]
        ds_out = {}
        do_out = {}
        for tc in range(32):
            s, q = tc // 4, tc % 4
            buf = OSTG[tc % 2]
            tw = [("Vc", v_xfin)]
            if tc >= 2:
                tw.append(ds_out[tc - 2])
                tw.append(("DO", do_out[tc - 2]))
            slots = []
            tv = 0
            for c in range(NCH):
                bank = pb[4 * (tc % 2) + c // 4]
                slot = bank[:, (c % 4) * 128:(c % 4) * 128 + 128]
                tv = top(lambda e, o=slot,
                         i=xt[(s, c)][:, q * 128:(q + 1) * 128]:
                         e.matmul(o, i, IDF[:, :], is_transpose=True,
                                  start=True, stop=True), waits=tw)
                tw = ()
                slots.append(slot)
            for c in range(NCH):
                P.op("vector", lambda e, o=buf[:, c * 128:(c + 1) * 128], i=slots[c]:
                     e.tensor_copy(o, i), waits=[("Tc", tv)], inc="DO")
            do_out[tc] = P.val["DO"]
            ds_out[tc] = dma("sync", out_h[tc * 128:(tc + 1) * 128, :], buf[:, :],
                             waits=[("DO", do_out[tc])])
        final_waits = [(f"DS{i}", P.val[f"DS{i}"]) for i in range(8)]
        final_waits += [(f"DG{i}", P.val[f"DG{i}"]) for i in range(4)]
        P.op("sync", lambda e: e.nop(), waits=final_waits)

        @block.sync
        def _(eng):
            for f in P.streams["sync"]:
                f(eng)

        @block.gpsimd
        def _(eng):
            for f in P.streams["gpsimd"]:
                f(eng)

        @block.vector
        def _(eng):
            for f in P.streams["vector"]:
                f(eng)

        @block.scalar
        def _(eng):
            for f in P.streams["scalar"]:
                f(eng)

        @block.tensor
        def _(eng):
            for f in P.streams["tensor"]:
                f(eng)

    return nc


def host_layouts(Wq, Wk, Wv, Wp, Wf, gamma, beta, bp):
    def pack_qkv(W):
        a = np.ascontiguousarray(np.asarray(W, dtype=np.float32)).reshape(8, 2, 8, 128, 64)
        return np.ascontiguousarray(a.transpose(3, 0, 2, 1, 4).reshape(128, 8192))

    fp = np.arange(1024)
    h_, e_ = fp // 64, fp % 64
    wp_perm = np.asarray(Wp, dtype=np.float32)[e_ * 16 + h_]
    wp_h = np.ascontiguousarray(wp_perm.reshape(8, 128, 1024).transpose(1, 0, 2)
                                .reshape(128, 8192))
    wfT = np.ascontiguousarray(np.asarray(Wf, dtype=np.float32).T)
    wf_h = np.ascontiguousarray(wfT.reshape(8, 128, 1024).transpose(1, 0, 2)
                                .reshape(128, 8192))
    zo = np.zeros((128, 2), dtype=ml_dtypes.bfloat16)
    zo[:, 1] = 1
    to_t = lambda v: np.ascontiguousarray(np.asarray(v, np.float32).reshape(8, 128).T)
    return dict(
        wq=pack_qkv(Wq), wk=pack_qkv(Wk), wv=pack_qkv(Wv), wp=wp_h, wf=wf_h,
        gt=to_t(gamma), bt=to_t(beta), bpv=to_t(bp),
        idf=np.eye(128, dtype=np.float32),
        idb=np.eye(128, dtype=ml_dtypes.bfloat16),
        onc=np.ones((1, 128), dtype=np.float32),
        on128=np.ones((128, 128), dtype=ml_dtypes.bfloat16),
        zo=zo,
    )


def kernel(L0_attn_query, L0_attn_Wq, L0_attn_Wk, L0_attn_Wv, L0_attn_Wp,
           L0_attn_bp, L0_ffn_weight, L0_ffn_gamma, L0_ffn_beta):
    common = host_layouts(L0_attn_Wq, L0_attn_Wk, L0_attn_Wv, L0_attn_Wp,
                          L0_ffn_weight, L0_ffn_gamma, L0_ffn_beta, L0_attn_bp)
    x = np.ascontiguousarray(np.asarray(L0_attn_query, dtype=np.float32))
    nc = build_nc()
    in_maps = [dict(common, x=np.ascontiguousarray(x[b])) for b in range(NCORES)]
    res = run_bass_kernel_spmd(nc, in_maps, core_ids=list(range(NCORES)))
    return np.stack([np.asarray(res.results[b]["out"]) for b in range(NCORES)], axis=0)
